# version 41
# baseline (speedup 1.0000x reference)
from contextlib import ExitStack

import numpy as np

import concourse.bass as bass
import concourse.mybir as mybir
from concourse.bass_utils import run_bass_kernel_spmd

F32 = mybir.dt.float32
BF16 = mybir.dt.bfloat16
AF = mybir.ActivationFunctionType
ALU = mybir.AluOpType
AX = mybir.AxisListType

N_CORES = 8
D = 1024
NCH = 8
NH = 8
S_TOTAL = 16384
OWN = S_TOTAL // N_CORES
HALO = 128
EXT = OWN + HALO
XCOLS = EXT + 1
W = 256
BLK = 128
NBLK = EXT // BLK

EPS = 1e-5

ENGS = ("sync", "tensor", "vector", "scalar", "gpsimd")


class Plan:

    def __init__(self):
        self.ops = {e: [] for e in ENGS}
        self.cnt = {}
        self.seq = {e: 0 for e in ENGS}

    def _bump(self, sem, amt):
        self.cnt[sem] = self.cnt.get(sem, 0) + amt
        return self.cnt[sem]

    def op(self, eng, fn, after=(), inc=None, inc_amt=1, self_inc=False):
        waits = []
        same_dep = None
        for t in after:
            if t is None:
                continue
            (peng, pseq, sem, val) = t
            if peng == eng and sem not in ("ld", "ld0", "ld1", "sp", "sp0", "sp1", "spL", "wl", "wl2", "wl3", "cc", "lc", "gg0", "gg1", "gg2", "gg3"):
                if same_dep is None or pseq > same_dep:
                    same_dep = pseq
            else:
                waits.append((sem, val))
        wmax = {}
        for (sem, val) in waits:
            if val > wmax.get(sem, -1):
                wmax[sem] = val
        waits = sorted(wmax.items())
        seq = self.seq[eng]
        self.seq[eng] += 1
        tok = None
        incs = []
        if inc is not None:
            val = self._bump(inc, inc_amt)
            tok = (eng, seq, inc, val)
            if not self_inc:
                incs = [(inc, inc_amt)]
        self.ops[eng].append((waits, same_dep, fn, incs, seq))
        return tok

    def wait(self, eng, sem, val):
        seq = self.seq[eng]
        self.seq[eng] += 1
        self.ops[eng].append(([(sem, val)], None, None, [], seq))

    def barrier(self):
        for e in ENGS:
            if e == "sync":
                waits = [("ld", self.cnt.get("ld", 0)),
                         ("ld0", self.cnt.get("ld0", 0)),
                         ("ld1", self.cnt.get("ld1", 0)),
                         ("wl", self.cnt.get("wl", 0))]
                self.op(e, lambda eng: eng.nop(), after=[], inc="bar")
                self.ops[e][-1] = (waits, None, lambda eng: eng.nop(),
                                   [("bar", 1)], self.ops[e][-1][4])
            elif e == "gpsimd":
                waits = [("sp", self.cnt.get("sp", 0)),
                         ("sp0", self.cnt.get("sp0", 0)),
                         ("sp1", self.cnt.get("sp1", 0)),
                         ("wl3", self.cnt.get("wl3", 0))]
                self.op(e, lambda eng: eng.nop(), after=[], inc="bar")
                self.ops[e][-1] = (waits, None, lambda eng: eng.nop(),
                                   [("bar", 1)], self.ops[e][-1][4])
            else:
                self.op(e, lambda eng: eng.drain(), after=[], inc="bar")
        total = self.cnt["bar"]
        for e in ENGS:
            self.wait(e, "bar", total)

    def emit(self, sems, engines):
        for ename in ENGS:
            oplist = self.ops[ename]
            if not oplist:
                continue

            def make(fname, olist):
                def emit_engine(eng):
                    watermark = {}
                    drained_upto = -1
                    for (waits, same_dep, fn, incs, seq) in olist:
                        for (sem, val) in waits:
                            if val > watermark.get(sem, 0):
                                eng.wait_ge(sems[sem], val)
                                watermark[sem] = val
                        if same_dep is not None and same_dep > drained_upto:
                            eng.drain()
                            drained_upto = seq - 1
                        if fn is None:
                            continue
                        ins = fn(eng)
                        for (sem, amt) in incs:
                            ins.then_inc(sems[sem], amt)
                return emit_engine

            engines[ename](make(ename, oplist))



def build_nc():
    nc = bass.Bass("TRN2", target_bir_lowering=False)

    x_fm = nc.dram_tensor("x", [D, XCOLS], F32, kind="ExternalInput")
    w_r = nc.dram_tensor("w_r", [D, D], F32, kind="ExternalInput")
    w_k = nc.dram_tensor("w_k", [D, D], F32, kind="ExternalInput")
    w_v = nc.dram_tensor("w_v", [D, D], F32, kind="ExternalInput")
    w_q = nc.dram_tensor("w_q", [D, D], F32, kind="ExternalInput")
    w_ka = nc.dram_tensor("w_ka", [D, D], F32, kind="ExternalInput")
    w_va = nc.dram_tensor("w_va", [D, D], F32, kind="ExternalInput")
    w_o = nc.dram_tensor("w_o", [D, D], F32, kind="ExternalInput")
    muv_d = nc.dram_tensor("muv", [128, 16], F32, kind="ExternalInput")
    nrmw_d = nc.dram_tensor("nrmw", [128, 128], F32, kind="ExternalInput")
    triu_d = nc.dram_tensor("triu", [128, 128], F32, kind="ExternalInput")
    idn_d = nc.dram_tensor("idn", [128, 128], F32, kind="ExternalInput")
    cst_d = nc.dram_tensor("cst", [128, 8], F32, kind="ExternalInput")
    onesr_d = nc.dram_tensor("onesr", [1, 128], F32, kind="ExternalInput")
    pmk_d = nc.dram_tensor("pmk", [128, 8], F32, kind="ExternalInput")

    out_fm = nc.dram_tensor("out", [D, OWN], F32, kind="ExternalOutput")

    u_sp = nc.dram_tensor("u_sp", [128, NCH * EXT], F32)
    q_sp = nc.dram_tensor("q_sp", [128, NCH * EXT], F32)
    kf_sp = nc.dram_tensor("kf_sp", [128, NCH * EXT], F32)
    kt_sp = nc.dram_tensor("kt_sp", [128, NBLK * NH * 128], F32)
    vt_sp = nc.dram_tensor("vt_sp", [128, NBLK * NH * 129], F32)
    v2_sp = nc.dram_tensor("v2_sp", [128, NCH * EXT], F32)
    cc_in = nc.dram_tensor("cc_in", [128, 1032], F32)
    cc_out = nc.dram_tensor("cc_out", [N_CORES * 128, 1032], F32,
                        addr_space="Shared")

    x_r = x_fm.ap().rearrange("(c p) j -> p c j", p=128)
    out_r = out_fm.ap().rearrange("(c p) j -> p c j", p=128)
    u_r = u_sp.ap().rearrange("p (c j) -> p c j", c=NCH)
    q_r = q_sp.ap().rearrange("p (c j) -> p c j", c=NCH)
    kf_r = kf_sp.ap().rearrange("p (c j) -> p c j", c=NCH)
    v2_r = v2_sp.ap().rearrange("p (c j) -> p c j", c=NCH)

    sem_names = ["ld", "ld0", "ld1", "sp", "sp0", "sp1", "spL", "mm", "vx", "ac", "cc", "bar", "wl", "wl2", "wl3", "lc", "gg0", "gg1", "gg2", "gg3"]

    with ExitStack() as stack:
        sems = {s: stack.enter_context(nc.semaphore(s)) for s in sem_names}

        wbuf = stack.enter_context(nc.sbuf_tensor("wbuf", [128, 3 * 2 * NCH * D],
                                                  BF16))
        cst = stack.enter_context(nc.sbuf_tensor("cstb", [128, 8], F32))
        muv = stack.enter_context(nc.sbuf_tensor("muvb", [128, 16], F32))
        nrmw = stack.enter_context(nc.sbuf_tensor("nrmwb", [128, 128], F32))
        triu = stack.enter_context(nc.sbuf_tensor("triub", [128, 128], F32))
        idn = stack.enter_context(nc.sbuf_tensor("idnb", [128, 128], F32))
        pmk = stack.enter_context(nc.sbuf_tensor("pmkb", [128, 8], F32))
        onesr = stack.enter_context(nc.sbuf_tensor("onesrb", [1, 128], F32))
        rowb = stack.enter_context(nc.sbuf_tensor("rowb", [1, 768], F32))
        bcz = stack.enter_context(nc.sbuf_tensor("bcz", [128, 512], F32))
        ccp = stack.enter_context(nc.sbuf_tensor("ccp", [128, 1032], F32))
        s0a = stack.enter_context(nc.sbuf_tensor("s0a", [128, 1032], F32))
        sml = stack.enter_context(nc.sbuf_tensor("sml", [128, 48], F32))

        wscr = [stack.enter_context(nc.sbuf_tensor("wscr%d" % i,
                                                    [128, D], F32))
                for i in range(3)]
        PS = [stack.enter_context(nc.psum_tensor("ps%d" % i, [128, 512], F32))
              for i in range(8)]

        ONES = cst[:, 0:1]
        EPSC = cst[0:1, 1:2]

        block = stack.enter_context(nc.Block())
        engines = {
            "sync": block.sync, "tensor": block.tensor, "vector": block.vector,
            "scalar": block.scalar, "gpsimd": block.gpsimd,
        }

        plan = Plan()
        P = plan.op
        bank_tok = [None] * 8
        G = {"v": None, "s": None, "lnq": None}

        scope_stack = []

        def stack_sbuf(name, shape, dt=F32):
            return scope_stack[-1].enter_context(nc.sbuf_tensor(name, shape, dt))

        def open_scope():
            scope_stack.append(ExitStack())

        def close_scope():
            scope_stack.pop().close()

        def dma(eng, dst, src):
            return eng.dma_start(out=dst, in_=src)

        def dma_nc(eng, dst, src):
            return eng.dma_start(out=dst, in_=src,
                                 allow_slow_non_contiguous=True)

        t_cst = P("sync", lambda e: [
            dma(e, cst[:, :], cst_d[:, :]).then_inc(sems["ld"], 16),
            dma(e, muv[:, :], muv_d[:, :]).then_inc(sems["ld"], 16),
            dma(e, nrmw[:, :], nrmw_d[:, :]).then_inc(sems["ld"], 16),
            dma(e, triu[:, :], triu_d[:, :]).then_inc(sems["ld"], 16),
            dma(e, idn[:, :], idn_d[:, :]).then_inc(sems["ld"], 16),
            dma(e, pmk[:, :], pmk_d[:, :]).then_inc(sems["ld"], 16),
            dma(e, onesr[:, :], onesr_d[:, :]).then_inc(sems["ld"], 16),
        ][-1], inc="ld", inc_amt=7 * 16, self_inc=True)

        t_zero = P("vector", lambda e: [
            e.memset(ccp[:, :], 0.0),
            e.memset(bcz[:, :], 0.0),
        ][-1], inc="vx")

        wsplit_state = {"prev": [None, None, None]}

        def lw_begin(pairs, gates=None, lanes=(0, 1, 2)):
            chunks = []
            for (slot, wd) in pairs:
                srcw = wd.ap().rearrange("(d p) e -> p d e", p=128)
                for d in range(NCH):
                    chunks.append((slot, srcw, d))
            return {"chunks": chunks, "i": 0, "tok": {}, "lanes": lanes,
                    "gates": gates if gates is not None else {}}

        def lw_pump(st, n):
            prev = wsplit_state["prev"]
            for _ in range(n):
                i = st["i"]
                if i >= len(st["chunks"]):
                    return
                st["i"] = i + 1
                (slot, srcw, d) = st["chunks"][i]
                buf = st["lanes"][i % len(st["lanes"])]
                qeng = ("sync", "scalar", "gpsimd")[buf]
                wsem = ("wl", "wl2", "wl3")[buf]
                t_l = P(qeng, lambda e, srcw=srcw, d=d, buf=buf: dma(
                    e, wscr[buf].ap().rearrange("p (a e) -> p a e", a=1),
                    srcw[:, d: d + 1, :]),
                    after=[prev[buf]], inc=wsem, inc_amt=16)
                base = (slot * 2) * NCH * D + d * D
                basel = (slot * 2 + 1) * NCH * D + d * D
                t_split = P("vector", lambda e, base=base, basel=basel,
                            buf=buf: [
                    e.tensor_copy(wbuf[:, base: base + D],
                                  wscr[buf][:, :]),
                    e.drain(),
                    e.tensor_sub(wbuf[:, basel: basel + D],
                                 wscr[buf][:, :],
                                 wbuf[:, base: base + D]),
                ][-1], after=[t_l, st["gates"].get(slot)], inc="vx")
                prev[buf] = t_split
                st["tok"][slot] = t_split

        def wsl(slot, half, d, e0, ew):
            base = (slot * 2 + half) * NCH * D + d * D + e0
            return wbuf[:, base: base + ew]

        def mm3(e, pv, slot, e0, ew, hv, lv, Wt):
            combos = [(dc, half, xv_) for dc in range(NCH)
                      for (half, xv_) in ((0, hv), (0, lv), (1, hv))]
            last = None
            for (i, (dc, half, xv_)) in enumerate(combos):
                last = e.matmul(pv, wsl(slot, half, dc, e0, ew),
                                xv_(dc, Wt), start=(i == 0),
                                stop=(i == len(combos) - 1))
            return last

        def emit_ln_stats(Wt, xs_view, sq_view, dep_in):
            t_sq = P("scalar", lambda e: [
                e.activation(sq_view(c), xs_view(c), AF.Square)
                for c in range(NCH)][-1],
                after=dep_in + [G["s"], G.get("lnq"), G.get("spl")], inc="ac")
            t_m = P("tensor", lambda e: [
                e.matmul(PS[0][0:1, 0:Wt], ONES, xs_view(c),
                         start=(c == 0), stop=(c == NCH - 1))
                for c in range(NCH)][-1],
                after=dep_in + [bank_tok[0]], inc="mm")
            t_q = P("tensor", lambda e: [
                e.matmul(PS[1][0:1, 0:Wt], ONES, sq_view(c),
                         start=(c == 0), stop=(c == NCH - 1))
                for c in range(NCH)][-1], after=[t_sq, bank_tok[1]], inc="mm")
            G["lnq"] = t_q
            rm = rowb[0:1, 0:Wt]
            rq = rowb[0:1, 256:256 + Wt]
            rs = rowb[0:1, 512:512 + Wt]
            rsd = rq
            t_rows = P("vector", lambda e: [
                e.tensor_scalar_mul(rm, PS[0][0:1, 0:Wt], 1.0 / D),
                e.tensor_scalar_mul(rq, PS[1][0:1, 0:Wt], 1.0 / D),
                e.drain(),
                e.tensor_mul(rs, rm, rm),
                e.drain(),
                e.tensor_sub(rs, rq, rs),
            ][-1], after=[t_m, t_q, G["v"]], inc="vx")
            t_sd = P("scalar", lambda e: e.activation(rsd, rs, AF.Sqrt, bias=EPSC),
                     after=[t_rows], inc="ac")
            t_si = P("vector", lambda e: [
                e.reciprocal(rs, rsd),
                e.drain(),
                e.tensor_mul(rq, rm, rs),
            ][-1], after=[t_sd], inc="vx")
            bank_tok[0] = t_rows
            bank_tok[1] = t_rows
            G["s"] = t_sd
            return (Wt, xs_view, t_si, t_rows, rs, rq)

        def emit_ln_apply(ctx, ys_view):
            (Wt, xs_view, t_si, t_rows, rs, rq) = ctx
            t_hl = P("vector", lambda e: [
                e.tensor_copy(bcz[0:1, 0:Wt], rs),
                e.tensor_copy(bcz[0:1, 256:256 + Wt], rq),
            ][-1], after=[t_si], inc="vx")
            t_bc = P("tensor", lambda e: e.matmul(
                PS[2][:, 0:512], triu[:, :], bcz[:, :],
                start=True, stop=True),
                after=[t_hl, bank_tok[2]], inc="mm")
            t_y = P("vector", lambda e: ([
                e.tensor_mul(ys_view(c), xs_view(c), PS[2][:, 0:Wt])
                for c in range(NCH)]
                + [e.drain()]
                + [e.tensor_sub(ys_view(c), ys_view(c),
                                PS[2][:, 256:256 + Wt])
                   for c in range(NCH)])[-1],
                after=[t_bc, t_rows], inc="vx")
            bank_tok[2] = t_y
            G["v"] = t_y
            return t_y

        def emit_ln(Wt, xs_view, ys_view, sq_view, dep_in):
            ctx = emit_ln_stats(Wt, xs_view, sq_view, dep_in)
            return emit_ln_apply(ctx, ys_view)

        def emit_ln_col(pc0, ypc, scratch8, dep_in):
            t_sq = P("scalar", lambda e: e.activation(scratch8, pc0, AF.Square),
                     after=dep_in + [G["s"]], inc="ac")
            t_mm = P("tensor", lambda e: [
                e.matmul(PS[0][0:1, 0:8], ONES, pc0, start=True, stop=True),
                e.matmul(PS[1][0:1, 0:8], ONES, scratch8, start=True, stop=True),
            ][-1], after=dep_in + [t_sq, bank_tok[0], bank_tok[1]], inc="mm")
            ra = rowb[0:1, 0:1]
            rb = rowb[0:1, 1:2]
            rv = rowb[0:1, 2:3]
            t_row = P("vector", lambda e: [
                e.reduce_sum(ra, PS[0][0:1, 0:8], axis=AX.X),
                e.reduce_sum(rb, PS[1][0:1, 0:8], axis=AX.X),
                e.drain(),
                e.tensor_scalar_mul(ra, ra, 1.0 / D),
                e.tensor_scalar_mul(rb, rb, 1.0 / D),
                e.drain(),
                e.tensor_mul(rv, ra, ra),
                e.drain(),
                e.tensor_sub(rv, rb, rv),
            ][-1], after=[t_mm, G["v"]], inc="vx")
            t_sd = P("scalar", lambda e: e.activation(rb, rv, AF.Sqrt, bias=EPSC),
                     after=[t_row], inc="ac")
            t_pack = P("vector", lambda e: [
                e.reciprocal(rowb[0:1, 5:6], rb),
                e.tensor_copy(rowb[0:1, 4:5], ra),
            ][-1], after=[t_sd], inc="vx")
            t_b = P("tensor", lambda e: e.matmul(
                PS[0][:, 0:2], onesr[0:1, :], rowb[0:1, 4:6],
                start=True, stop=True), after=[t_pack, t_row], inc="mm")
            t_e = P("vector", lambda e: e.tensor_copy(sml[:, 8:10], PS[0][:, 0:2]),
                    after=[t_b], inc="vx")
            t_y = P("vector", lambda e: e.tensor_scalar(
                ypc, pc0, sml[:, 8:9], sml[:, 9:10], ALU.subtract, ALU.mult),
                after=[t_e], inc="vx")
            bank_tok[0] = t_e
            bank_tok[1] = t_row
            G["s"] = t_sd
            G["v"] = t_y
            return t_y

        def ffn_sweep(tag, tiles, ypc_fn, src_r, out_is_final,
                      weights_fn=None, smap=(0, 1, 2)):
            xsb = [stack_sbuf("xs%s%d" % (tag, i), [128, NCH * W])
                   for i in range(2)]
            sqb = stack_sbuf("sq" + tag, [128, NCH * W])
            ys = stack_sbuf("ys" + tag, [128, NCH * (W + 1)])
            tm1 = sqb
            tm2 = stack_sbuf("tm2" + tag, [128, NCH * W])
            tm3 = stack_sbuf("tm3" + tag, [128, NCH * W])
            tm4 = stack_sbuf("tm4" + tag, [128, NCH * W])
            scr = stack_sbuf("scr" + tag, [128, W])
            h1 = stack_sbuf("h1" + tag, [128, NCH * W], BF16)
            l1 = stack_sbuf("l1" + tag, [128, NCH * W], BF16)
            h2 = stack_sbuf("h2" + tag, [128, NCH * W], BF16)
            l2 = stack_sbuf("l2" + tag, [128, NCH * W], BF16)
            h4 = stack_sbuf("h4" + tag, [128, NCH * W], BF16)
            l4 = stack_sbuf("l4" + tag, [128, NCH * W], BF16)

            xs3b = [x.ap().rearrange("p (c j) -> p c j", c=NCH) for x in xsb]
            ys3 = ys.ap().rearrange("p (c j) -> p c j", c=NCH)

            def xv(par, c, Wt):
                return xsb[par][:, c * W: c * W + Wt]

            def yv(c, Wt):
                return ys[:, c * (W + 1) + 1: c * (W + 1) + 1 + Wt]

            def ypv(c, Wt):
                return ys[:, c * (W + 1): c * (W + 1) + Wt]

            def t1v(c, Wt):
                return tm1[:, c * W: c * W + Wt]

            def t2v(c, Wt):
                return tm2[:, c * W: c * W + Wt]

            def t3v(c, Wt):
                return tm3[:, c * W: c * W + Wt]

            def t4v(c, Wt):
                return tm4[:, c * W: c * W + Wt]

            def sqv(c, Wt):
                return sqb[:, c * W: c * W + Wt]

            def bv(buf):
                def f(c, Wt):
                    return buf[:, c * W: c * W + Wt]
                return f
            h1v, l1v, h2v, l2v, h4v, l4v = map(bv, (h1, l1, h2, l2, h4, l4))

            t_ypc = ypc_fn()
            t_carry = P("vector", lambda e: e.tensor_copy(
                ys3[:, :, 0:1],
                sml[:, 0:8].rearrange("p (c o) -> p c o", o=1)),
                after=[t_ypc], inc="vx")

            spill_by_par = [None, None]
            t_spill_prev = None

            def tile_load(ti):
                (s0, Wt) = tiles[ti]
                par = ti % 2
                if out_is_final:
                    srca = src_r[:, :, s0: s0 + Wt]
                else:
                    srca = src_r[:, :, s0 + 1: s0 + 1 + Wt]
                return P("sync", lambda e, srca=srca, Wt=Wt, par=par: dma(
                    e, xs3b[par][:, :, 0:Wt], srca),
                    after=[spill_by_par[par]], inc="ld%d" % par, inc_amt=16)

            def tile_ln_stats(ti, t_x):
                (s0, Wt) = tiles[ti]
                par = ti % 2
                return emit_ln_stats(Wt,
                                     lambda c, Wt=Wt, par=par: xv(par, c, Wt),
                                     lambda c, Wt=Wt: sqv(c, Wt), [t_x])

            def tile_ln_apply(ctx):
                Wt = ctx[0]
                return emit_ln_apply(ctx, lambda c, Wt=Wt: yv(c, Wt))

            def tile_blend(ti, t_y, kmm_gate):
                (s0, Wt) = tiles[ti]

                def blend_r(e, Wt=Wt):
                    for c in range(NCH):
                        e.tensor_sub(t2v(c, Wt), yv(c, Wt), ypv(c, Wt))
                    e.drain()
                    for c in range(NCH):
                        e.scalar_tensor_tensor(
                            t1v(c, Wt), t2v(c, Wt), muv[:, c: c + 1],
                            ypv(c, Wt), ALU.mult, ALU.add)
                    e.drain()
                    if Wt == W:
                        e.tensor_copy(h1[:, :], tm1[:, :])
                        e.drain()
                        return e.tensor_sub(l1[:, :], tm1[:, :], h1[:, :])
                    for c in range(NCH):
                        e.tensor_copy(h1v(c, Wt), t1v(c, Wt))
                    e.drain()
                    last = None
                    for c in range(NCH):
                        last = e.tensor_sub(l1v(c, Wt), t1v(c, Wt),
                                            h1v(c, Wt))
                    return last
                t_sr = P("vector", blend_r, after=[t_y, kmm_gate], inc="vx")

                def blend_k(e, Wt=Wt):
                    for c in range(NCH):
                        e.scalar_tensor_tensor(
                            t2v(c, Wt), t2v(c, Wt), muv[:, 8 + c: 9 + c],
                            ypv(c, Wt), ALU.mult, ALU.add)
                    e.drain()
                    if Wt == W:
                        e.tensor_copy(h2[:, :], tm2[:, :])
                        e.drain()
                        return e.tensor_sub(l2[:, :], tm2[:, :], h2[:, :])
                    for c in range(NCH):
                        e.tensor_copy(h2v(c, Wt), t2v(c, Wt))
                    e.drain()
                    last = None
                    for c in range(NCH):
                        last = e.tensor_sub(l2v(c, Wt), t2v(c, Wt),
                                            h2v(c, Wt))
                    return last
                def emit_k():
                    t_sk = P("vector", blend_k, after=[t_sr], inc="vx")
                    P("vector", lambda e, Wt=Wt: e.tensor_copy(
                        ys3[:, :, 0:1], ys3[:, :, Wt: Wt + 1]),
                        after=[t_sk], inc="vx")
                    G["spl"] = t_sk
                    return t_sk
                return (t_sr, emit_k)

            t_x = tile_load(0)
            wst = weights_fn()
            wtoks = wst["tok"]
            last_mm = {"r": None, "k": None, "v": None}
            pending_y = tile_ln_apply(tile_ln_stats(0, t_x))
            (t_sr0, emit_k0) = tile_blend(0, pending_y, None)
            pending_spl = (t_sr0, emit_k0())
            lw_pump(wst, NCH)
            pending_ctx = None
            for (ti, (s0, Wt)) in enumerate(tiles):
                par = ti % 2
                t_y = pending_y
                t_spl = pending_spl

                t_sig = {}
                for ec in range(NCH):
                    pb = 4 + (ec % 4)
                    t_mm = P("tensor", lambda e, ec=ec, pb=pb, Wt=Wt:
                             mm3(e, PS[pb][:, 0:Wt], smap[0], ec * 128, 128,
                                 h1v, l1v, Wt),
                             after=[t_spl[0], wtoks[smap[0]],
                                    bank_tok[pb]],
                             inc="mm")
                    last_mm["r"] = t_mm
                    t_sig[ec] = P("scalar", lambda e, ec=ec, pb=pb, Wt=Wt:
                                  e.activation(t3v(ec, Wt), PS[pb][:, 0:Wt],
                                               AF.Sigmoid),
                                  after=[t_mm], inc="ac")
                    bank_tok[pb] = t_sig[ec]

                if ti == 0:
                    lw_pump(wst, NCH)
                if ti + 1 < len(tiles):
                    t_xn = tile_load(ti + 1)
                    pending_ctx = tile_ln_stats(ti + 1, t_xn)

                t_kact = {}
                t_ksub = {}
                kmm_last = None
                for ec in range(NCH):
                    pb = 4 + (ec % 4)
                    t_mm = P("tensor", lambda e, ec=ec, pb=pb, Wt=Wt:
                             mm3(e, PS[pb][:, 0:Wt], smap[1], ec * 128, 128,
                                 h2v, l2v, Wt),
                             after=[t_spl[1], wtoks[smap[1]],
                                    bank_tok[pb]],
                             inc="mm")
                    kmm_last = t_mm
                    last_mm["k"] = t_mm
                    t_kact[ec] = P("scalar", lambda e, ec=ec, pb=pb, Wt=Wt: [
                        e.activation(t4v(ec, Wt), PS[pb][:, 0:Wt], AF.Relu),
                        e.drain(),
                        e.activation(t4v(ec, Wt), t4v(ec, Wt), AF.Square),
                        e.drain(),
                        e.activation(h4v(ec, Wt), t4v(ec, Wt), AF.Identity),
                    ][-1], after=[t_mm], inc="ac")
                    t_ksub[ec] = P("vector", lambda e, ec=ec, Wt=Wt:
                                   e.tensor_sub(l4v(ec, Wt), t4v(ec, Wt),
                                                h4v(ec, Wt)),
                                   after=[t_kact[ec]], inc="vx")
                    bank_tok[pb] = t_kact[ec]
                if ti == 0:
                    lw_pump(wst, NCH)
                pending_k_fn = None
                if ti + 1 < len(tiles):
                    pending_y = tile_ln_apply(pending_ctx)
                    (t_srn, pending_k_fn) = tile_blend(ti + 1, pending_y,
                                                       kmm_last)

                t_combs = []
                t_cb_prev = None
                for ec in range(NCH):
                    pb = 4 + (ec % 4)
                    P("tensor", lambda e, ec=ec, pb=pb, Wt=Wt: [
                        e.matmul(PS[pb][:, 0:Wt],
                                 wsl(smap[2], 0, dc, ec * 128, 128),
                                 h4v(dc, Wt), start=(dc == 0), stop=False)
                        for dc in range(NCH)][-1],
                        after=[t_kact[NCH - 1], wtoks[smap[2]], bank_tok[pb]])
                    t_mm = P("tensor", lambda e, ec=ec, pb=pb, Wt=Wt: [
                        e.matmul(PS[pb][:, 0:Wt],
                                 wsl(smap[2], hf, dc, ec * 128, 128),
                                 xv_(dc, Wt),
                                 start=False,
                                 stop=(dc == NCH - 1 and hf == 1))
                        for (hf, xv_) in ((0, l4v), (1, h4v))
                        for dc in range(NCH)][-1],
                        after=[t_ksub[NCH - 1]], inc="mm")
                    last_mm["v"] = t_mm
                    t_cb = P("vector", lambda e, ec=ec, pb=pb, Wt=Wt, par=par: [
                        e.tensor_mul(scr[:, 0:Wt], t3v(ec, Wt), PS[pb][:, 0:Wt]),
                        e.drain(),
                        e.scalar_tensor_tensor(
                            xv(par, ec, Wt), scr[:, 0:Wt], 0.5, xv(par, ec, Wt),
                            ALU.mult, ALU.add),
                    ][-1], after=[t_mm, t_y, t_sig[ec], t_cb_prev], inc="vx")
                    bank_tok[pb] = t_cb
                    t_cb_prev = t_cb
                    t_combs.append(t_cb)
                if out_is_final:
                    dst = out_r[:, :, s0 - HALO: s0 - HALO + Wt]
                else:
                    dst = u_r[:, :, s0: s0 + Wt]
                t_spill_prev = P("gpsimd", lambda e, dst=dst, Wt=Wt, par=par: dma(
                    e, dst, xs3b[par][:, :, 0:Wt]),
                    after=t_combs, inc="sp", inc_amt=16)
                spill_by_par[par] = t_spill_prev
                if pending_k_fn is not None:
                    pending_spl = (t_srn, pending_k_fn())
                G["v"] = t_combs[-1]
                G["s"] = t_kact[NCH - 1]
            last_mm["spill"] = t_spill_prev
            return last_mm

        open_scope()

        def ypc_A():
            pc0 = sml[:, 0:8]
            sc8 = sml[:, 16:24]
            t_l = P("sync", lambda e: dma_nc(e, pc0, x_r[:, :, 0:1].rearrange(
                "p c o -> p (c o)")), inc="ld", inc_amt=16)
            return emit_ln_col(pc0, pc0, sc8, [t_l, t_zero])

        tilesA = [(0, HALO)] + [(HALO + W * k, W) for k in range(OWN // W)]
        resA = ffn_sweep("A", tilesA, ypc_A, x_r, False,
                         lambda: lw_begin([(0, w_r), (1, w_k), (2, w_v)]))
        wstB = lw_begin([(1, w_ka), (2, w_va), (0, w_q)],
                        gates={1: resA["k"], 0: resA["r"], 2: resA["v"]})
        lw_pump(wstB, NCH)
        plan.barrier()
        close_scope()

        open_scope()

        usb = [stack_sbuf("us%d" % i, [128, NCH * W]) for i in range(2)]
        us3b = [u.ap().rearrange("p (c j) -> p c j", c=NCH) for u in usb]
        y2s = stack_sbuf("y2s", [128, NCH * W])
        y2h = [stack_sbuf("y2h%d" % i, [128, NCH * W], BF16) for i in range(2)]
        y2l = [stack_sbuf("y2l%d" % i, [128, NCH * W], BF16) for i in range(2)]
        sqln = stack_sbuf("sqln", [128, NCH * W])
        qev = stack_sbuf("qev", [128, NCH * W])
        ks = stack_sbuf("ks", [128, NCH * W])
        ktb = stack_sbuf("ktb", [128, (W // 128) * NH * 128])
        vtb = stack_sbuf("vtb", [128, (W // 128) * NH * 129])
        vtb3 = vtb.ap().rearrange("p (g f) -> p g f", f=129)
        qev3 = qev.ap().rearrange("p (c j) -> p c j", c=NCH)
        ks3 = ks.ap().rearrange("p (c j) -> p c j", c=NCH)

        def uvB(par, c, Wt):
            return usb[par][:, c * W: c * W + Wt]

        def y2hv(par):
            def f(c, Wt):
                return y2h[par][:, c * W: c * W + Wt]
            return f

        def y2lv(par):
            def f(c, Wt):
                return y2l[par][:, c * W: c * W + Wt]
            return f

        def sqvB(c, Wt):
            return sqln[:, c * W: c * W + Wt]

        def qvB(c, Wt):
            return qev[:, c * W: c * W + Wt]

        def kvB(c, Wt):
            return ks[:, c * W: c * W + Wt]

        y_by_par = [None, None]
        split_guard = [None, None]
        sp_tile = None
        t_acc_all = []
        deferred_q = []
        late_q_spills = []
        t_vones = P("vector", lambda e: e.memset(
            vtb3[:, :, 128:129], 1.0), inc="vx")

        def b_load(ti):
            (s0, Wt) = tilesA[ti]
            par = ti % 2
            return P("sync", lambda e, s0=s0, Wt=Wt, par=par: dma(
                e, us3b[par][:, :, 0:Wt], u_r[:, :, s0: s0 + Wt]),
                after=[y_by_par[par]], inc="ld%d" % par, inc_amt=16)

        def b_ln_stats(ti, t_u, extra):
            (s0, Wt) = tilesA[ti]
            par = ti % 2
            return emit_ln_stats(Wt, lambda c, Wt=Wt, par=par: uvB(par, c, Wt),
                                 lambda c, Wt=Wt: sqvB(c, Wt), [t_u] + extra)

        def b_ln_apply(ctx, ti):
            (s0, Wt) = tilesA[ti]
            par = ti % 2
            t_y = emit_ln_apply(ctx, lambda c, Wt=ctx[0]:
                                y2s[:, c * W: c * W + Wt])
            if Wt == W:
                t_spl = P("vector", lambda e, par=par: [
                    e.tensor_copy(y2h[par][:, :], y2s[:, :]),
                    e.drain(),
                    e.tensor_sub(y2l[par][:, :], y2s[:, :], y2h[par][:, :]),
                ][-1], after=[t_y, split_guard[par]], inc="vx")
            else:
                t_spl = P("vector", lambda e, par=par, Wt=Wt: ([
                    e.tensor_copy(y2hv(par)(c, Wt),
                                  y2s[:, c * W: c * W + Wt])
                    for c in range(NCH)]
                    + [e.drain()]
                    + [e.tensor_sub(y2lv(par)(c, Wt),
                                    y2s[:, c * W: c * W + Wt],
                                    y2hv(par)(c, Wt))
                       for c in range(NCH)])[-1],
                    after=[t_y, split_guard[par]], inc="vx")
            return t_spl

        t_u = b_load(0)
        wtoksB = wstB["tok"]
        pending_y = b_ln_apply(b_ln_stats(0, t_u, []), 0)
        bk_last = {}
        pending_ctx = None
        for (ti, (s0, Wt)) in enumerate(tilesA):
            par = ti % 2
            nb = Wt // 128
            b0 = s0 // 128
            t_y = pending_y
            y_by_par[par] = t_y
            t_ke = {}
            for ec in range(NCH):
                pb = 4 + (ec % 4)
                t_mm = P("tensor", lambda e, ec=ec, pb=pb, Wt=Wt, par=par:
                         mm3(e, PS[pb][:, 0:Wt], 1, ec * 128, 128,
                             y2hv(par), y2lv(par), Wt),
                         after=[t_y, wtoksB[1], bank_tok[pb]], inc="mm")
                bk_last["k"] = t_mm
                t_ke[ec] = P("vector", lambda e, ec=ec, pb=pb, Wt=Wt:
                             e.tensor_copy(kvB(ec, Wt), PS[pb][:, 0:Wt]),
                             after=[t_mm, sp_tile], inc="vx")
                bank_tok[pb] = t_ke[ec]
            if ti == 0:
                lw_pump(wstB, NCH)
            if ti + 1 < len(tilesA):
                t_un = b_load(ti + 1)
                pending_ctx = b_ln_stats(ti + 1, t_un, [])
            t_ve = {}
            for j in range(nb):
                for eh in range(2):
                    pb = 4 + ((j * 2 + eh) % 4)

                    def vfn(e, j=j, eh=eh, pb=pb, par=par):
                        combos = [(dc, sb, wh) for dc in range(NCH)
                                  for (sb, wh) in ((y2h, 0), (y2l, 0),
                                                   (y2h, 1))]
                        last = None
                        for (i, (dc, sb, wh)) in enumerate(combos):
                            last = e.matmul(
                                PS[pb][:, 0:512],
                                sb[par][:, dc * W + j * 128:
                                        dc * W + (j + 1) * 128],
                                wsl(2, wh, dc, eh * 512, 512),
                                start=(i == 0), stop=(i == len(combos) - 1))
                        return last
                    t_mm = P("tensor", vfn,
                             after=[t_y, wtoksB[2], bank_tok[pb]], inc="mm")
                    bk_last["v"] = t_mm
                    t_ve[(j, eh)] = P("vector", lambda e, j=j, eh=eh, pb=pb:
                                      e.tensor_copy(
                        vtb3[:, j * NH + eh * 4: j * NH + eh * 4 + 4, 0:128],
                        PS[pb][:, 0:512].rearrange("p (h f) -> p h f", f=128)),
                        after=[t_mm, sp_tile, t_vones], inc="vx")
                    bank_tok[pb] = t_ve[(j, eh)]
            if ti == 0:
                lw_pump(wstB, NCH)
            t_kt = {}
            for (ix, (h, j)) in enumerate([(h, j) for h in range(NH)
                                           for j in range(nb)]):
                pb = 6 + (ix % 2)
                t_tr = P("tensor", lambda e, h=h, j=j, pb=pb: e.transpose(
                    PS[pb][:, 0:128],
                    ks[:, h * W + j * 128: h * W + (j + 1) * 128],
                    idn[:, :]),
                    after=[t_ke[h], bank_tok[pb]], inc="mm")
                t_kt[(h, j)] = P("vector", lambda e, h=h, j=j, pb=pb:
                                 e.tensor_copy(
                    ktb[:, (j * NH + h) * 128: (j * NH + h + 1) * 128],
                    PS[pb][:, 0:128]), after=[t_tr, sp_tile], inc="vx")
                bank_tok[pb] = t_kt[(h, j)]
            if s0 >= HALO:
                for h in range(NH):
                    pb = 6 + (h % 2)
                    t_tm = P("tensor", lambda e, h=h, nb=nb, pb=pb: [
                        e.matmul(PS[pb][:, 0:129],
                                 ktb[:, (j * NH + h) * 128:
                                     (j * NH + h + 1) * 128],
                                 vtb3[:, j * NH + h, 0:129],
                                 start=(j == 0), stop=(j == nb - 1))
                        for j in range(nb)][-1],
                        after=[bank_tok[pb]]
                        + [t_kt[(h, j)] for j in range(nb)]
                        + [t_ve[(j, eh)] for j in range(nb) for eh in range(2)],
                        inc="mm")
                    t_acc = P("vector", lambda e, h=h, pb=pb: e.tensor_add(
                        ccp[:, h * 129: (h + 1) * 129],
                        ccp[:, h * 129: (h + 1) * 129], PS[pb][:, 0:129]),
                        after=[t_tm, t_zero], inc="vx")
                    bank_tok[pb] = t_acc
                    t_acc_all.append(t_acc)
                G["v"] = t_acc
            else:
                G["v"] = t_kt[(NH - 1, nb - 1)]
            if ti + 1 < len(tilesA):
                pending_y = b_ln_apply(pending_ctx, ti + 1)
            def emit_q(par, s0, Wt, t_y, qgate=None,
                       spq=("gpsimd", "sp")):
                t_qe = {}
                t_mm = None
                for ec in range(NCH):
                    pb = 4 + (ec % 4)
                    t_mm = P("tensor", lambda e, ec=ec, pb=pb, Wt=Wt, par=par:
                             mm3(e, PS[pb][:, 0:Wt], 0, ec * 128, 128,
                                 y2hv(par), y2lv(par), Wt),
                             after=[t_y, wtoksB[0], bank_tok[pb]], inc="mm")
                    t_qe[ec] = P("vector", lambda e, ec=ec, pb=pb, Wt=Wt:
                                 e.tensor_copy(qvB(ec, Wt), PS[pb][:, 0:Wt]),
                                 after=[t_mm, sp_tile, qgate], inc="vx")
                    bank_tok[pb] = t_qe[ec]
                t_qs = P(spq[0], lambda e, s0=s0, Wt=Wt: dma(
                    e, q_r[:, :, s0: s0 + Wt], qev3[:, :, 0:Wt]),
                    after=[t_qe[c] for c in range(NCH)], inc=spq[1],
                    inc_amt=16)
                return t_mm, t_qs

            if ti < len(tilesA) - 2:
                t_mmq, _ = emit_q(par, s0, Wt, t_y)
                split_guard[par] = t_mmq
            else:
                deferred_q.append((par, s0, Wt, t_y))
            last_tile = ti == len(tilesA) - 1
            wsem = "spL" if last_tile else "sp"

            def sp_kf(s0=s0, Wt=Wt, t_ke=t_ke, wsem=wsem):
                return P("gpsimd", lambda e: dma(
                    e, kf_r[:, :, s0: s0 + Wt], ks3[:, :, 0:Wt]),
                    after=[t_ke[c] for c in range(NCH)], inc=wsem,
                    inc_amt=16)

            def sp_kt(b0=b0, nb=nb, t_kt=t_kt, wsem=wsem):
                return P("gpsimd", lambda e: dma(
                    e, kt_sp[:, b0 * NH * 128: (b0 + nb) * NH * 128],
                    ktb[:, 0: nb * NH * 128]),
                    after=[t_kt[(h, j)] for h in range(NH)
                           for j in range(nb)],
                    inc=wsem, inc_amt=16)

            def sp_vt(b0=b0, nb=nb, t_ve=t_ve, wsem=wsem):
                return P("gpsimd", lambda e: dma(
                    e, vt_sp[:, b0 * NH * 129: (b0 + nb) * NH * 129],
                    vtb[:, 0: nb * NH * 129]),
                    after=[t_vones]
                    + [t_ve[(j, eh)] for j in range(nb) for eh in range(2)],
                    inc=wsem, inc_amt=16)

            if last_tile:
                late_b_spills = [sp_kf, sp_kt, sp_vt]
                sp_tile = None
            else:
                sp_kf()
                sp_kt()
                sp_tile = sp_vt()

        t_pack = None
        for h in range(NH):
            t_pack = P("gpsimd", lambda e, h=h: dma(
                e, cc_in[:, h * 129: (h + 1) * 129],
                ccp[:, h * 129: (h + 1) * 129]),
                after=[t_acc_all[-NH + h], t_pack], inc="sp0", inc_amt=16)
        t_cc = P("gpsimd", lambda e: e.collective_compute(
            "AllGather", ALU.bypass, replica_groups=[list(range(N_CORES))],
            ins=[cc_in.ap().opt()], outs=[cc_out.ap().opt()]),
            after=[t_pack], inc="cc")
        t_qs = None
        t_mmq_last = None
        for (parq, s0q, Wtq, t_yq) in deferred_q:
            t_mmq_last, t_qs = emit_q(parq, s0q, Wtq, t_yq, qgate=t_qs)
            late_q_spills.append(t_qs)
        for fn in late_b_spills:
            late_q_spills.append(fn())
        wstC = lw_begin([(1, w_o)], gates={1: bk_last["k"]},
                        lanes=(0, 1))
        lw_pump(wstC, NCH)
        wstD = lw_begin([(2, w_k), (0, w_r), (1, w_v)],
                        gates={2: bk_last["v"], 0: t_mmq_last},
                        lanes=(0, 1))
        lw_pump(wstD, NCH)
        plan.barrier()
        close_scope()

        open_scope()
        t_w = wstC["tok"][1]
        t_sz = P("vector", lambda e: e.memset(s0a[:, :], 0.0),
                 after=[G["v"]], inc="vx")
        gball = stack_sbuf("gball", [128, 4 * 2064])
        t_gls = []
        for half in range(4):
            t_gls.append(P("scalar", lambda e, half=half: dma(
                e, gball[:, half * 2064: (half + 1) * 2064]
                .rearrange("p (j f) -> p j f", j=2),
                cc_out.ap().rearrange("(j p) f -> p j f", p=128)
                [:, half * 2: (half + 1) * 2, :]).then_inc(
                    sems["gg%d" % half], 16),
                after=[t_cc], inc="gg%d" % half, inc_amt=16, self_inc=True))
        t_acc_prev = t_sz
        for half in range(4):
            for j in range(2):
                t_acc_prev = P("vector", lambda e, j=j, half=half:
                               e.scalar_tensor_tensor(
                    s0a[:, :],
                    gball[:, half * 2064 + j * 1032:
                          half * 2064 + (j + 1) * 1032],
                    pmk[:, half * 2 + j: half * 2 + j + 1], s0a[:, :],
                    ALU.mult, ALU.add),
                    after=[t_gls[half], t_acc_prev], inc="vx")
        t_s0 = t_acc_prev

        qb = [stack_sbuf("qb%d" % i, [128, NH * 128]) for i in range(2)]
        kfb = [stack_sbuf("kfb%d" % i, [128, NH * 128]) for i in range(2)]
        ktc = [stack_sbuf("ktc%d" % i, [128, NH * 128]) for i in range(2)]
        vtc = [stack_sbuf("vtc%d" % i, [128, NH * 129]) for i in range(2)]
        msT = [stack_sbuf("msT%d" % i, [128, 128]) for i in range(2)]
        o1b = [stack_sbuf("o1b%d" % i, [128, 128]) for i in range(2)]
        oscr = [stack_sbuf("oscr%d" % i, [128, 128]) for i in range(3)]
        ofmh = stack_sbuf("ofmh", [128, NH * 128], BF16)
        ofml = stack_sbuf("ofml", [128, NH * 128], BF16)
        v2b = [stack_sbuf("v2b%d" % i, [128, NCH * 128]) for i in range(2)]
        updsb = stack_sbuf("updsb", [128, 2 * 129])
        def S129(h):
            return s0a[:, h * 129: (h + 1) * 129]
        UPD = [PS[0], PS[1]]
        SC = [PS[2], PS[3]]
        OD = [PS[4], PS[5]]
        TR = [PS[6], PS[7]]
        DABS = [sml[:, 26:27], sml[:, 32:33]]
        DMAX = [sml[:, 27:28], sml[:, 33:34]]
        DRCP = [sml[:, 28:29], sml[:, 34:35]]
        DSS = [sml[:, 29:30], sml[:, 35:36]]
        DSD = [sml[:, 30:31], sml[:, 36:37]]
        DRMS = [sml[:, 31:32], sml[:, 37:38]]

        def emit_loads(b, after_toks):
            par = b % 2
            s0 = b * BLK
            ldp = "ld%d" % par

            def fn(e):
                dma(e, qb[par].ap().rearrange("p (c j) -> p c j", c=NCH),
                    q_r[:, :, s0: s0 + BLK]).then_inc(sems[ldp], 16)
                dma(e, kfb[par].ap().rearrange("p (c j) -> p c j", c=NCH),
                    kf_r[:, :, s0: s0 + BLK]).then_inc(sems[ldp], 16)
                dma(e, ktc[par][:, :],
                    kt_sp[:, b * NH * 128: (b + 1) * NH * 128]
                    ).then_inc(sems[ldp], 16)
                dma(e, vtc[par][:, :],
                    vt_sp[:, b * NH * 129: (b + 1) * NH * 129]
                    ).then_inc(sems[ldp], 16)
                last = dma(e, v2b[par].ap().rearrange("p (c j) -> p c j",
                                                      c=NCH),
                           u_r[:, :, s0: s0 + BLK])
                last.then_inc(sems[ldp], 16)
                return last
            extra = late_q_spills if (b + 1) * BLK > EXT - 2 * W else []
            return P("sync", fn, after=after_toks + extra, inc=ldp,
                     inc_amt=5 * 16, self_inc=True)

        state_tok = [t_s0] * NH
        hb = {
            "updev": [None, None],
            "mask": [None, None],
            "o1": [None, None],
            "abs": [None, None],
            "ofm": [None, None],
            "o2": [None, None],
            "rcp": [None, None],
            "troscr": [None, None, None],
        }
        t_loads = {0: emit_loads(0, []), 1: emit_loads(1, [])}
        t_consumed = {0: [], 1: []}
        t_spv2_prev = None

        def hsl(buf, h):
            return buf[:, h * 128: (h + 1) * 128]

        def vsl(buf, h):
            return buf[:, h * 129: h * 129 + 129]

        for b in range(NBLK):
            par = b % 2
            s0 = b * BLK
            ld1 = t_loads[b]
            q_, kf_, kt_, vt_ = qb[par], kfb[par], ktc[par], vtc[par]

            t_o_h = {}
            t_tr_h = {}
            t_ofm = {}

            halo = (b == 0)

            def emit_upd_sc(h):
                hp = h % 2
                t_upd = P("tensor", lambda e, kt_=kt_, vt_=vt_, h=h, hp=hp:
                          e.matmul(UPD[hp][:, 0:129], hsl(kt_, h),
                                   vsl(vt_, h), start=True, stop=True),
                          after=[ld1, hb["updev"][hp]], inc="mm")
                t_uev = P("vector", lambda e, hp=hp: e.tensor_copy(
                    updsb[:, hp * 129: (hp + 1) * 129], UPD[hp][:, 0:129]),
                    after=[t_upd], inc="vx")
                hb["updev"][hp] = t_uev
                sdep = state_tok[h]
                if halo:
                    t_sub = P("vector", lambda e, h=h, hp=hp: e.tensor_sub(
                        S129(h), S129(h),
                        updsb[:, hp * 129: (hp + 1) * 129]),
                        after=[t_uev, sdep, t_s0], inc="vx")
                    sdep = t_sub
                t_sc = P("tensor", lambda e, kf_=kf_, q_=q_, h=h, hp=hp:
                         e.matmul(SC[hp][:, 0:128], hsl(kf_, h), hsl(q_, h),
                                  start=True, stop=True),
                         after=[ld1, hb["mask"][hp]], inc="mm")
                t_ms = P("vector", lambda e, hp=hp: e.tensor_mul(
                    msT[hp][:, :], SC[hp][:, 0:128], triu[:, :]),
                    after=[t_sc], inc="vx")
                hb["mask"][hp] = t_ms
                return (t_uev, t_ms, sdep)

            def emit_o(h, toks):
                hp = h % 2
                (t_upd, t_ms, sdep) = toks
                t_o = P("tensor", lambda e, q_=q_, vt_=vt_, h=h, hp=hp: [
                    e.matmul(OD[hp][:, 0:129], msT[hp][:, :], vsl(vt_, h),
                             start=True, stop=False),
                    e.matmul(OD[hp][:, 0:129], hsl(q_, h), S129(h),
                             start=False, stop=True),
                ][-1], after=[t_ms, ld1, sdep, hb["o1"][hp], hb["abs"][hp]],
                    inc="mm")
                t_o_h[h] = t_o
                t_sadd = P("vector", lambda e, h=h, hp=hp: e.tensor_add(
                    S129(h), S129(h),
                    updsb[:, hp * 129: (hp + 1) * 129]),
                    after=[t_o, t_upd], inc="vx")
                state_tok[h] = t_sadd
                t_abs = P("scalar", lambda e, hp=hp: e.activation(
                    DABS[hp], OD[hp][:, 128:129], AF.Abs),
                    after=[t_o], inc="ac")
                hb["abs"][hp] = t_abs
                t_rcp = P("vector", lambda e, hp=hp: [
                    e.tensor_scalar_max(DMAX[hp], DABS[hp], 1.0),
                    e.drain(),
                    e.reciprocal(DRCP[hp], DMAX[hp]),
                ][-1], after=[t_abs, hb["rcp"][hp]], inc="vx")
                hb["rcp"][hp] = t_rcp
                t_o1 = P("scalar", lambda e, hp=hp: e.activation(
                    o1b[hp][:, :], OD[hp][:, 0:128], AF.Copy,
                    scale=DRCP[hp]),
                    after=[t_o, t_rcp, hb["o2"][hp]], inc="ac")
                hb["o1"][hp] = t_o1
                h3 = h % 3
                t_sd = P("scalar", lambda e, hp=hp, h3=h3: [
                    e.activation(oscr[h3][:, :], o1b[hp][:, :], AF.Square,
                                 accum_out=DSS[hp]),
                    e.drain(),
                    e.activation(DSD[hp], DSS[hp], AF.Sqrt,
                                 bias=cst[:, 1:2], scale=cst[:, 3:4]),
                ][-1], after=[t_o1, hb["troscr"][h3]], inc="ac")
                t_o2 = P("vector", lambda e, hp=hp, h3=h3: [
                    e.reciprocal(DRMS[hp], DSD[hp]),
                    e.drain(),
                    e.scalar_tensor_tensor(
                        oscr[h3][:, :], o1b[hp][:, :], DRMS[hp],
                        nrmw[:, :], ALU.mult, ALU.mult),
                ][-1], after=[t_sd, hb["troscr"][h3]], inc="vx")
                hb["o2"][hp] = t_o2
                return t_o2

            def emit_tr(h, t_o2):
                hp = h % 2
                h3 = h % 3
                t_tr = P("tensor", lambda e, hp=hp, h3=h3: e.transpose(
                    TR[hp][:, 0:128], oscr[h3][:, :], idn[:, :]),
                    after=[t_o2, hb["ofm"][hp]], inc="mm")
                t_tr_h[h] = t_tr
                hb["troscr"][h3] = t_tr
                t_oh = P("scalar", lambda e, h=h, hp=hp: e.activation(
                    hsl(ofmh, h), TR[hp][:, 0:128], AF.Copy),
                    after=[t_tr], inc="ac")
                t_ofm[h] = P("vector", lambda e, h=h, hp=hp: e.tensor_sub(
                    hsl(ofml, h), TR[hp][:, 0:128], hsl(ofmh, h)),
                    after=[t_oh], inc="vx")
                hb["ofm"][hp] = t_ofm[h]

            toks = {}
            o2s = {}
            for h in range(NH):
                toks[h] = emit_upd_sc(h)
                if h >= 1:
                    o2s[h - 1] = emit_o(h - 1, toks[h - 1])
                if h >= 3:
                    emit_tr(h - 3, o2s[h - 3])
            o2s[NH - 1] = emit_o(NH - 1, toks[NH - 1])
            emit_tr(NH - 3, o2s[NH - 3])
            emit_tr(NH - 2, o2s[NH - 2])
            emit_tr(NH - 1, o2s[NH - 1])

            t_v2 = {}
            for dc in range(NCH):
                hp = dc % 2

                def wofn(e, dc=dc, hp=hp):
                    combos = [(ec, sb, wh) for ec in range(NCH)
                              for (sb, wh) in ((ofmh, 0), (ofml, 0),
                                               (ofmh, 1))]
                    last = None
                    for (i, (ec, sb, wh)) in enumerate(combos):
                        last = e.matmul(OD[hp][:, 0:128],
                                        wsl(1, wh, ec, dc * 128, 128),
                                        hsl(sb, ec), start=(i == 0),
                                        stop=(i == len(combos) - 1))
                    return last
                t_mm = P("tensor", wofn,
                         after=[t_w, hb["o1"][hp], hb["abs"][hp],
                                t_v2.get(dc - 2)]
                         + [t_ofm[h] for h in range(NH)],
                         inc="mm")
                G["wo_last"] = t_mm
                t_v2[dc] = P("vector", lambda e, dc=dc, hp=hp, par=par:
                             e.tensor_add(
                    v2b[par][:, dc * 128: (dc + 1) * 128],
                    v2b[par][:, dc * 128: (dc + 1) * 128],
                    OD[hp][:, 0:128]),
                    after=[t_mm, ld1, t_spv2_prev], inc="vx")
            hb["o1"] = [t_v2[NCH - 2], t_v2[NCH - 1]]
            t_spv2_prev = P("gpsimd", lambda e, s0=s0, par=par: dma(
                e, v2_r[:, :, s0: s0 + BLK],
                v2b[par].ap().rearrange("p (c j) -> p c j", c=NCH)),
                after=[t_v2[dc] for dc in range(NCH)], inc="sp", inc_amt=16)
            G["v"] = t_v2[NCH - 1]
            t_consumed[par] = [t_spv2_prev] + [t_ofm[h] for h in range(NH)] \
                + [t_v2[dc] for dc in range(NCH)] + [t_o_h[NH - 1]]
            if b + 2 < NBLK:
                t_loads[b + 2] = emit_loads(b + 2, t_consumed[par])
            if b == 1:
                lw_pump(wstD, NCH)
        plan.barrier()
        close_scope()

        open_scope()

        def ypc_D():
            pc0 = sml[:, 0:8]
            sc8 = sml[:, 16:24]
            t_l = P("sync", lambda e: dma_nc(
                e, pc0, v2_r[:, :, HALO - 1: HALO].rearrange(
                    "p c o -> p (c o)")), inc="ld", inc_amt=16)
            return emit_ln_col(pc0, pc0, sc8, [t_l])

        tilesD = [(HALO + W * k, W) for k in range(OWN // W)]
        wstD["gates"][1] = G["wo_last"]
        ffn_sweep("D", tilesD, ypc_D, v2_r, True,
                  lambda: wstD, smap=(0, 2, 1))
        plan.barrier()
        close_scope()

        plan.emit(sems, engines)

    return nc


_NC_CACHE = None


def _get_nc():
    global _NC_CACHE
    if _NC_CACHE is None:
        _NC_CACHE = build_nc()
    return _NC_CACHE


def make_in_maps(x, Wr, Wk, Wv, Wq, Wka, Wva, Wo, mu_r, mu_k, norm_w):
    def T(a):
        return np.ascontiguousarray(np.asarray(a, dtype=np.float32).T)

    w_r, w_k, w_v = T(Wr), T(Wk), T(Wv)
    w_q, w_ka, w_va, w_o = T(Wq), T(Wka), T(Wva), T(Wo)

    mu_r = np.asarray(mu_r, dtype=np.float32)
    mu_k = np.asarray(mu_k, dtype=np.float32)
    muv = np.empty((128, 16), np.float32)
    muv[:, 0:8] = mu_r.reshape(NCH, 128).T
    muv[:, 8:16] = mu_k.reshape(NCH, 128).T

    nw = np.asarray(norm_w, dtype=np.float32)
    nrmw = np.ascontiguousarray(np.broadcast_to(nw[None, :], (128, 128)),
                                dtype=np.float32)
    triu = np.triu(np.ones((128, 128), np.float32))
    idn = np.eye(128, dtype=np.float32)
    cst = np.zeros((128, 8), np.float32)
    cst[:, 0] = 1.0
    cst[:, 1] = EPS
    cst[:, 2] = 1.0 / D
    cst[:, 3] = 1.0 / 128
    cst[:, 4] = 0.5
    onesr = np.ones((1, 128), np.float32)

    in_maps = []
    for i in range(N_CORES):
        lo = i * OWN - (HALO + 1)
        x_ext = np.zeros((XCOLS, D), np.float32)
        src_lo = max(lo, 0)
        x_ext[src_lo - lo: XCOLS, :] = x[src_lo: (i + 1) * OWN, :]
        pmk = np.zeros((128, 8), np.float32)
        pmk[:, :i] = 1.0
        in_maps.append({
            "x": np.ascontiguousarray(x_ext.T),
            "w_r": w_r, "w_k": w_k, "w_v": w_v,
            "w_q": w_q, "w_ka": w_ka, "w_va": w_va, "w_o": w_o,
            "muv": muv, "nrmw": nrmw, "triu": triu, "idn": idn,
            "cst": cst, "onesr": onesr, "pmk": pmk,
        })
    return in_maps


def kernel(L0_attn_hidden_states, L0_attn_Wq, L0_attn_Wk, L0_attn_Wv,
           L0_attn_Wo, L0_attn_norm_w, L0_ffn_W_r, L0_ffn_W_k, L0_ffn_W_v,
           L0_ffn_mu_r, L0_ffn_mu_k):
    x = np.asarray(L0_attn_hidden_states, dtype=np.float32).reshape(S_TOTAL, D)
    in_maps = make_in_maps(
        x, L0_ffn_W_r, L0_ffn_W_k, L0_ffn_W_v,
        L0_attn_Wq, L0_attn_Wk, L0_attn_Wv, L0_attn_Wo,
        L0_ffn_mu_r, L0_ffn_mu_k, L0_attn_norm_w)
    nc = _get_nc()
    res = run_bass_kernel_spmd(nc, in_maps, core_ids=list(range(N_CORES)))
    out = np.empty((S_TOTAL, D), np.float32)
    for i in range(N_CORES):
        out[i * OWN: (i + 1) * OWN, :] = res.results[i]["out"].T
    return out.reshape(1, S_TOTAL, D)


# version 42
# speedup vs baseline: 1.0358x; 1.0358x over previous
from contextlib import ExitStack

import numpy as np

import concourse.bass as bass
import concourse.mybir as mybir
from concourse.bass_utils import run_bass_kernel_spmd

F32 = mybir.dt.float32
BF16 = mybir.dt.bfloat16
AF = mybir.ActivationFunctionType
ALU = mybir.AluOpType
AX = mybir.AxisListType

N_CORES = 8
D = 1024
NCH = 8
NH = 8
S_TOTAL = 16384
OWN = S_TOTAL // N_CORES
HALO = 128
EXT = OWN + HALO
XCOLS = EXT + 1
W = 256
BLK = 128
NBLK = EXT // BLK

EPS = 1e-5

ENGS = ("sync", "tensor", "vector", "scalar", "gpsimd")


class Plan:

    def __init__(self):
        self.ops = {e: [] for e in ENGS}
        self.cnt = {}
        self.seq = {e: 0 for e in ENGS}

    def _bump(self, sem, amt):
        self.cnt[sem] = self.cnt.get(sem, 0) + amt
        return self.cnt[sem]

    def op(self, eng, fn, after=(), inc=None, inc_amt=1, self_inc=False):
        waits = []
        same_dep = None
        for t in after:
            if t is None:
                continue
            (peng, pseq, sem, val) = t
            if peng == eng and sem not in ("ld", "ld0", "ld1", "sp", "sp0", "sp1", "spL", "wl", "wl2", "wl3", "cc", "lc", "gg0", "gg1", "gg2", "gg3"):
                if same_dep is None or pseq > same_dep:
                    same_dep = pseq
            else:
                waits.append((sem, val))
        wmax = {}
        for (sem, val) in waits:
            if val > wmax.get(sem, -1):
                wmax[sem] = val
        waits = sorted(wmax.items())
        seq = self.seq[eng]
        self.seq[eng] += 1
        tok = None
        incs = []
        if inc is not None:
            val = self._bump(inc, inc_amt)
            tok = (eng, seq, inc, val)
            if not self_inc:
                incs = [(inc, inc_amt)]
        self.ops[eng].append((waits, same_dep, fn, incs, seq))
        return tok

    def wait(self, eng, sem, val):
        seq = self.seq[eng]
        self.seq[eng] += 1
        self.ops[eng].append(([(sem, val)], None, None, [], seq))

    def barrier(self):
        for e in ENGS:
            if e == "sync":
                waits = [("ld", self.cnt.get("ld", 0)),
                         ("ld0", self.cnt.get("ld0", 0)),
                         ("ld1", self.cnt.get("ld1", 0)),
                         ("wl", self.cnt.get("wl", 0))]
                self.op(e, lambda eng: eng.nop(), after=[], inc="bar")
                self.ops[e][-1] = (waits, None, lambda eng: eng.nop(),
                                   [("bar", 1)], self.ops[e][-1][4])
            elif e == "gpsimd":
                waits = [("sp", self.cnt.get("sp", 0)),
                         ("sp0", self.cnt.get("sp0", 0)),
                         ("sp1", self.cnt.get("sp1", 0)),
                         ("wl3", self.cnt.get("wl3", 0))]
                self.op(e, lambda eng: eng.nop(), after=[], inc="bar")
                self.ops[e][-1] = (waits, None, lambda eng: eng.nop(),
                                   [("bar", 1)], self.ops[e][-1][4])
            else:
                self.op(e, lambda eng: eng.drain(), after=[], inc="bar")
        total = self.cnt["bar"]
        for e in ENGS:
            self.wait(e, "bar", total)

    def emit(self, sems, engines):
        for ename in ENGS:
            oplist = self.ops[ename]
            if not oplist:
                continue

            def make(fname, olist):
                def emit_engine(eng):
                    watermark = {}
                    drained_upto = -1
                    for (waits, same_dep, fn, incs, seq) in olist:
                        for (sem, val) in waits:
                            if val > watermark.get(sem, 0):
                                eng.wait_ge(sems[sem], val)
                                watermark[sem] = val
                        if same_dep is not None and same_dep > drained_upto:
                            eng.drain()
                            drained_upto = seq - 1
                        if fn is None:
                            continue
                        ins = fn(eng)
                        for (sem, amt) in incs:
                            ins.then_inc(sems[sem], amt)
                return emit_engine

            engines[ename](make(ename, oplist))



def build_nc():
    nc = bass.Bass("TRN2", target_bir_lowering=False)

    x_fm = nc.dram_tensor("x", [D, XCOLS], F32, kind="ExternalInput")
    w_r = nc.dram_tensor("w_r", [D, D], F32, kind="ExternalInput")
    w_k = nc.dram_tensor("w_k", [D, D], F32, kind="ExternalInput")
    w_v = nc.dram_tensor("w_v", [D, D], F32, kind="ExternalInput")
    w_q = nc.dram_tensor("w_q", [D, D], F32, kind="ExternalInput")
    w_ka = nc.dram_tensor("w_ka", [D, D], F32, kind="ExternalInput")
    w_va = nc.dram_tensor("w_va", [D, D], F32, kind="ExternalInput")
    w_o = nc.dram_tensor("w_o", [D, D], F32, kind="ExternalInput")
    muv_d = nc.dram_tensor("muv", [128, 16], F32, kind="ExternalInput")
    nrmw_d = nc.dram_tensor("nrmw", [128, 128], F32, kind="ExternalInput")
    triu_d = nc.dram_tensor("triu", [128, 128], F32, kind="ExternalInput")
    idn_d = nc.dram_tensor("idn", [128, 128], F32, kind="ExternalInput")
    cst_d = nc.dram_tensor("cst", [128, 8], F32, kind="ExternalInput")
    onesr_d = nc.dram_tensor("onesr", [1, 128], F32, kind="ExternalInput")
    pmk_d = nc.dram_tensor("pmk", [128, 8], F32, kind="ExternalInput")

    out_fm = nc.dram_tensor("out", [D, OWN], F32, kind="ExternalOutput")

    u_sp = nc.dram_tensor("u_sp", [128, NCH * EXT], F32)
    q_sp = nc.dram_tensor("q_sp", [128, NCH * EXT], F32)
    kf_sp = nc.dram_tensor("kf_sp", [128, NCH * EXT], F32)
    kt_sp = nc.dram_tensor("kt_sp", [128, NBLK * NH * 128], F32)
    vt_sp = nc.dram_tensor("vt_sp", [128, NBLK * NH * 129], F32)
    v2_sp = nc.dram_tensor("v2_sp", [128, NCH * EXT], F32)
    cc_in = nc.dram_tensor("cc_in", [128, 1032], F32)
    cc_out = nc.dram_tensor("cc_out", [N_CORES * 128, 1032], F32,
                        addr_space="Shared")

    x_r = x_fm.ap().rearrange("(c p) j -> p c j", p=128)
    out_r = out_fm.ap().rearrange("(c p) j -> p c j", p=128)
    u_r = u_sp.ap().rearrange("p (c j) -> p c j", c=NCH)
    q_r = q_sp.ap().rearrange("p (c j) -> p c j", c=NCH)
    kf_r = kf_sp.ap().rearrange("p (c j) -> p c j", c=NCH)
    v2_r = v2_sp.ap().rearrange("p (c j) -> p c j", c=NCH)

    sem_names = ["ld", "ld0", "ld1", "sp", "sp0", "sp1", "spL", "mm", "vx", "ac", "cc", "bar", "wl", "wl2", "wl3", "lc", "gg0", "gg1", "gg2", "gg3"]

    with ExitStack() as stack:
        sems = {s: stack.enter_context(nc.semaphore(s)) for s in sem_names}

        wbuf = stack.enter_context(nc.sbuf_tensor("wbuf", [128, 3 * 2 * NCH * D],
                                                  BF16))
        cst = stack.enter_context(nc.sbuf_tensor("cstb", [128, 8], F32))
        muv = stack.enter_context(nc.sbuf_tensor("muvb", [128, 16], F32))
        nrmw = stack.enter_context(nc.sbuf_tensor("nrmwb", [128, 128], F32))
        triu = stack.enter_context(nc.sbuf_tensor("triub", [128, 128], F32))
        idn = stack.enter_context(nc.sbuf_tensor("idnb", [128, 128], F32))
        pmk = stack.enter_context(nc.sbuf_tensor("pmkb", [128, 8], F32))
        onesr = stack.enter_context(nc.sbuf_tensor("onesrb", [1, 128], F32))
        rowb = stack.enter_context(nc.sbuf_tensor("rowb", [1, 768], F32))
        bcz = stack.enter_context(nc.sbuf_tensor("bcz", [128, 512], F32))
        ccp = stack.enter_context(nc.sbuf_tensor("ccp", [128, 1032], F32))
        s0a = stack.enter_context(nc.sbuf_tensor("s0a", [128, 1032], F32))
        sml = stack.enter_context(nc.sbuf_tensor("sml", [128, 48], F32))

        wscr = [stack.enter_context(nc.sbuf_tensor("wscr%d" % i,
                                                    [128, D], F32))
                for i in range(3)]
        PS = [stack.enter_context(nc.psum_tensor("ps%d" % i, [128, 512], F32))
              for i in range(8)]

        ONES = cst[:, 0:1]
        EPSC = cst[0:1, 1:2]

        block = stack.enter_context(nc.Block())
        engines = {
            "sync": block.sync, "tensor": block.tensor, "vector": block.vector,
            "scalar": block.scalar, "gpsimd": block.gpsimd,
        }

        plan = Plan()
        P = plan.op
        bank_tok = [None] * 8
        G = {"v": None, "s": None, "lnq": None}

        scope_stack = []

        def stack_sbuf(name, shape, dt=F32):
            return scope_stack[-1].enter_context(nc.sbuf_tensor(name, shape, dt))

        def open_scope():
            scope_stack.append(ExitStack())

        def close_scope():
            scope_stack.pop().close()

        def dma(eng, dst, src):
            return eng.dma_start(out=dst, in_=src)

        def dma_nc(eng, dst, src):
            return eng.dma_start(out=dst, in_=src,
                                 allow_slow_non_contiguous=True)

        t_cst = P("sync", lambda e: [
            dma(e, cst[:, :], cst_d[:, :]).then_inc(sems["ld"], 16),
            dma(e, muv[:, :], muv_d[:, :]).then_inc(sems["ld"], 16),
            dma(e, nrmw[:, :], nrmw_d[:, :]).then_inc(sems["ld"], 16),
            dma(e, triu[:, :], triu_d[:, :]).then_inc(sems["ld"], 16),
            dma(e, idn[:, :], idn_d[:, :]).then_inc(sems["ld"], 16),
            dma(e, pmk[:, :], pmk_d[:, :]).then_inc(sems["ld"], 16),
            dma(e, onesr[:, :], onesr_d[:, :]).then_inc(sems["ld"], 16),
        ][-1], inc="ld", inc_amt=7 * 16, self_inc=True)

        t_zero = P("vector", lambda e: [
            e.memset(ccp[:, :], 0.0),
            e.memset(bcz[:, :], 0.0),
        ][-1], inc="vx")

        wsplit_state = {"prev": [None, None, None]}

        def lw_begin(pairs, gates=None, lanes=(0, 1, 2)):
            chunks = []
            for (slot, wd) in pairs:
                srcw = wd.ap().rearrange("(d p) e -> p d e", p=128)
                for d in range(NCH):
                    chunks.append((slot, srcw, d))
            return {"chunks": chunks, "i": 0, "tok": {}, "lanes": lanes,
                    "gates": gates if gates is not None else {}}

        def lw_pump(st, n):
            prev = wsplit_state["prev"]
            for _ in range(n):
                i = st["i"]
                if i >= len(st["chunks"]):
                    return
                st["i"] = i + 1
                (slot, srcw, d) = st["chunks"][i]
                buf = st["lanes"][i % len(st["lanes"])]
                qeng = ("sync", "scalar", "gpsimd")[buf]
                wsem = ("wl", "wl2", "wl3")[buf]
                t_l = P(qeng, lambda e, srcw=srcw, d=d, buf=buf: dma(
                    e, wscr[buf].ap().rearrange("p (a e) -> p a e", a=1),
                    srcw[:, d: d + 1, :]),
                    after=[prev[buf]], inc=wsem, inc_amt=16)
                base = (slot * 2) * NCH * D + d * D
                basel = (slot * 2 + 1) * NCH * D + d * D
                t_split = P("vector", lambda e, base=base, basel=basel,
                            buf=buf: [
                    e.tensor_copy(wbuf[:, base: base + D],
                                  wscr[buf][:, :]),
                    e.drain(),
                    e.tensor_sub(wbuf[:, basel: basel + D],
                                 wscr[buf][:, :],
                                 wbuf[:, base: base + D]),
                ][-1], after=[t_l, st["gates"].get(slot)], inc="vx")
                prev[buf] = t_split
                st["tok"][slot] = t_split

        def wsl(slot, half, d, e0, ew):
            base = (slot * 2 + half) * NCH * D + d * D + e0
            return wbuf[:, base: base + ew]

        def mm3(e, pv, slot, e0, ew, hv, lv, Wt):
            combos = [(dc, half, xv_) for dc in range(NCH)
                      for (half, xv_) in ((0, hv), (0, lv), (1, hv))]
            last = None
            for (i, (dc, half, xv_)) in enumerate(combos):
                last = e.matmul(pv, wsl(slot, half, dc, e0, ew),
                                xv_(dc, Wt), start=(i == 0),
                                stop=(i == len(combos) - 1))
            return last

        def emit_ln_stats(Wt, xs_view, sq_view, dep_in):
            t_sq = P("scalar", lambda e: [
                e.activation(sq_view(c), xs_view(c), AF.Square)
                for c in range(NCH)][-1],
                after=dep_in + [G["s"], G.get("lnq"), G.get("spl")], inc="ac")
            t_m = P("tensor", lambda e: [
                e.matmul(PS[0][0:1, 0:Wt], ONES, xs_view(c),
                         start=(c == 0), stop=(c == NCH - 1))
                for c in range(NCH)][-1],
                after=dep_in + [bank_tok[0]], inc="mm")
            t_q = P("tensor", lambda e: [
                e.matmul(PS[1][0:1, 0:Wt], ONES, sq_view(c),
                         start=(c == 0), stop=(c == NCH - 1))
                for c in range(NCH)][-1], after=[t_sq, bank_tok[1]], inc="mm")
            G["lnq"] = t_q
            rm = rowb[0:1, 0:Wt]
            rq = rowb[0:1, 256:256 + Wt]
            rs = rowb[0:1, 512:512 + Wt]
            rsd = rq
            t_rows = P("vector", lambda e: [
                e.tensor_scalar_mul(rm, PS[0][0:1, 0:Wt], 1.0 / D),
                e.tensor_scalar_mul(rq, PS[1][0:1, 0:Wt], 1.0 / D),
                e.drain(),
                e.tensor_mul(rs, rm, rm),
                e.drain(),
                e.tensor_sub(rs, rq, rs),
            ][-1], after=[t_m, t_q, G["v"]], inc="vx")
            t_sd = P("scalar", lambda e: e.activation(rsd, rs, AF.Sqrt, bias=EPSC),
                     after=[t_rows], inc="ac")
            t_si = P("vector", lambda e: [
                e.reciprocal(rs, rsd),
                e.drain(),
                e.tensor_mul(rq, rm, rs),
            ][-1], after=[t_sd], inc="vx")
            bank_tok[0] = t_rows
            bank_tok[1] = t_rows
            G["s"] = t_sd
            return (Wt, xs_view, t_si, t_rows, rs, rq)

        def emit_ln_apply(ctx, ys_view):
            (Wt, xs_view, t_si, t_rows, rs, rq) = ctx
            t_hl = P("vector", lambda e: [
                e.tensor_copy(bcz[0:1, 0:Wt], rs),
                e.tensor_copy(bcz[0:1, 256:256 + Wt], rq),
            ][-1], after=[t_si], inc="vx")
            t_bc = P("tensor", lambda e: e.matmul(
                PS[2][:, 0:512], triu[:, :], bcz[:, :],
                start=True, stop=True),
                after=[t_hl, bank_tok[2]], inc="mm")
            t_y = P("vector", lambda e: ([
                e.tensor_mul(ys_view(c), xs_view(c), PS[2][:, 0:Wt])
                for c in range(NCH)]
                + [e.drain()]
                + [e.tensor_sub(ys_view(c), ys_view(c),
                                PS[2][:, 256:256 + Wt])
                   for c in range(NCH)])[-1],
                after=[t_bc, t_rows], inc="vx")
            bank_tok[2] = t_y
            G["v"] = t_y
            return t_y

        def emit_ln(Wt, xs_view, ys_view, sq_view, dep_in):
            ctx = emit_ln_stats(Wt, xs_view, sq_view, dep_in)
            return emit_ln_apply(ctx, ys_view)

        def emit_ln_col(pc0, ypc, scratch8, dep_in):
            t_sq = P("scalar", lambda e: e.activation(scratch8, pc0, AF.Square),
                     after=dep_in + [G["s"]], inc="ac")
            t_mm = P("tensor", lambda e: [
                e.matmul(PS[0][0:1, 0:8], ONES, pc0, start=True, stop=True),
                e.matmul(PS[1][0:1, 0:8], ONES, scratch8, start=True, stop=True),
            ][-1], after=dep_in + [t_sq, bank_tok[0], bank_tok[1]], inc="mm")
            ra = rowb[0:1, 0:1]
            rb = rowb[0:1, 1:2]
            rv = rowb[0:1, 2:3]
            t_row = P("vector", lambda e: [
                e.reduce_sum(ra, PS[0][0:1, 0:8], axis=AX.X),
                e.reduce_sum(rb, PS[1][0:1, 0:8], axis=AX.X),
                e.drain(),
                e.tensor_scalar_mul(ra, ra, 1.0 / D),
                e.tensor_scalar_mul(rb, rb, 1.0 / D),
                e.drain(),
                e.tensor_mul(rv, ra, ra),
                e.drain(),
                e.tensor_sub(rv, rb, rv),
            ][-1], after=[t_mm, G["v"]], inc="vx")
            t_sd = P("scalar", lambda e: e.activation(rb, rv, AF.Sqrt, bias=EPSC),
                     after=[t_row], inc="ac")
            t_pack = P("vector", lambda e: [
                e.reciprocal(rowb[0:1, 5:6], rb),
                e.tensor_copy(rowb[0:1, 4:5], ra),
            ][-1], after=[t_sd], inc="vx")
            t_b = P("tensor", lambda e: e.matmul(
                PS[0][:, 0:2], onesr[0:1, :], rowb[0:1, 4:6],
                start=True, stop=True), after=[t_pack, t_row], inc="mm")
            t_e = P("vector", lambda e: e.tensor_copy(sml[:, 8:10], PS[0][:, 0:2]),
                    after=[t_b], inc="vx")
            t_y = P("vector", lambda e: e.tensor_scalar(
                ypc, pc0, sml[:, 8:9], sml[:, 9:10], ALU.subtract, ALU.mult),
                after=[t_e], inc="vx")
            bank_tok[0] = t_e
            bank_tok[1] = t_row
            G["s"] = t_sd
            G["v"] = t_y
            return t_y

        def ffn_sweep(tag, tiles, ypc_fn, src_r, out_is_final,
                      weights_fn=None, smap=(0, 1, 2), tail_fn=None):
            xsb = [stack_sbuf("xs%s%d" % (tag, i), [128, NCH * W])
                   for i in range(2)]
            sqb = stack_sbuf("sq" + tag, [128, NCH * W])
            ys = stack_sbuf("ys" + tag, [128, NCH * (W + 1)])
            tm1 = sqb
            tm2 = stack_sbuf("tm2" + tag, [128, NCH * W])
            tm3 = stack_sbuf("tm3" + tag, [128, NCH * W])
            tm4 = stack_sbuf("tm4" + tag, [128, NCH * W])
            scr = stack_sbuf("scr" + tag, [128, W])
            h1 = stack_sbuf("h1" + tag, [128, NCH * W], BF16)
            l1 = stack_sbuf("l1" + tag, [128, NCH * W], BF16)
            h2 = stack_sbuf("h2" + tag, [128, NCH * W], BF16)
            l2 = stack_sbuf("l2" + tag, [128, NCH * W], BF16)
            h4 = stack_sbuf("h4" + tag, [128, NCH * W], BF16)
            l4 = stack_sbuf("l4" + tag, [128, NCH * W], BF16)

            xs3b = [x.ap().rearrange("p (c j) -> p c j", c=NCH) for x in xsb]
            ys3 = ys.ap().rearrange("p (c j) -> p c j", c=NCH)

            def xv(par, c, Wt):
                return xsb[par][:, c * W: c * W + Wt]

            def yv(c, Wt):
                return ys[:, c * (W + 1) + 1: c * (W + 1) + 1 + Wt]

            def ypv(c, Wt):
                return ys[:, c * (W + 1): c * (W + 1) + Wt]

            def t1v(c, Wt):
                return tm1[:, c * W: c * W + Wt]

            def t2v(c, Wt):
                return tm2[:, c * W: c * W + Wt]

            def t3v(c, Wt):
                return tm3[:, c * W: c * W + Wt]

            def t4v(c, Wt):
                return tm4[:, c * W: c * W + Wt]

            def sqv(c, Wt):
                return sqb[:, c * W: c * W + Wt]

            def bv(buf):
                def f(c, Wt):
                    return buf[:, c * W: c * W + Wt]
                return f
            h1v, l1v, h2v, l2v, h4v, l4v = map(bv, (h1, l1, h2, l2, h4, l4))

            t_ypc = ypc_fn()
            t_carry = P("vector", lambda e: e.tensor_copy(
                ys3[:, :, 0:1],
                sml[:, 0:8].rearrange("p (c o) -> p c o", o=1)),
                after=[t_ypc], inc="vx")

            spill_by_par = [None, None]
            t_spill_prev = None

            def tile_load(ti):
                (s0, Wt) = tiles[ti]
                par = ti % 2
                if out_is_final:
                    srca = src_r[:, :, s0: s0 + Wt]
                else:
                    srca = src_r[:, :, s0 + 1: s0 + 1 + Wt]
                return P("sync", lambda e, srca=srca, Wt=Wt, par=par: dma(
                    e, xs3b[par][:, :, 0:Wt], srca),
                    after=[spill_by_par[par]], inc="ld%d" % par, inc_amt=16)

            def tile_ln_stats(ti, t_x):
                (s0, Wt) = tiles[ti]
                par = ti % 2
                return emit_ln_stats(Wt,
                                     lambda c, Wt=Wt, par=par: xv(par, c, Wt),
                                     lambda c, Wt=Wt: sqv(c, Wt), [t_x])

            def tile_ln_apply(ctx):
                Wt = ctx[0]
                return emit_ln_apply(ctx, lambda c, Wt=Wt: yv(c, Wt))

            def tile_blend(ti, t_y, kmm_gate):
                (s0, Wt) = tiles[ti]

                def blend_r(e, Wt=Wt):
                    for c in range(NCH):
                        e.tensor_sub(t2v(c, Wt), yv(c, Wt), ypv(c, Wt))
                    e.drain()
                    for c in range(NCH):
                        e.scalar_tensor_tensor(
                            t1v(c, Wt), t2v(c, Wt), muv[:, c: c + 1],
                            ypv(c, Wt), ALU.mult, ALU.add)
                    e.drain()
                    if Wt == W:
                        e.tensor_copy(h1[:, :], tm1[:, :])
                        e.drain()
                        return e.tensor_sub(l1[:, :], tm1[:, :], h1[:, :])
                    for c in range(NCH):
                        e.tensor_copy(h1v(c, Wt), t1v(c, Wt))
                    e.drain()
                    last = None
                    for c in range(NCH):
                        last = e.tensor_sub(l1v(c, Wt), t1v(c, Wt),
                                            h1v(c, Wt))
                    return last
                t_sr = P("vector", blend_r, after=[t_y, kmm_gate], inc="vx")

                def blend_k(e, Wt=Wt):
                    for c in range(NCH):
                        e.scalar_tensor_tensor(
                            t2v(c, Wt), t2v(c, Wt), muv[:, 8 + c: 9 + c],
                            ypv(c, Wt), ALU.mult, ALU.add)
                    e.drain()
                    if Wt == W:
                        e.tensor_copy(h2[:, :], tm2[:, :])
                        e.drain()
                        return e.tensor_sub(l2[:, :], tm2[:, :], h2[:, :])
                    for c in range(NCH):
                        e.tensor_copy(h2v(c, Wt), t2v(c, Wt))
                    e.drain()
                    last = None
                    for c in range(NCH):
                        last = e.tensor_sub(l2v(c, Wt), t2v(c, Wt),
                                            h2v(c, Wt))
                    return last
                def emit_k():
                    t_sk = P("vector", blend_k, after=[t_sr], inc="vx")
                    P("vector", lambda e, Wt=Wt: e.tensor_copy(
                        ys3[:, :, 0:1], ys3[:, :, Wt: Wt + 1]),
                        after=[t_sk], inc="vx")
                    G["spl"] = t_sk
                    return t_sk
                return (t_sr, emit_k)

            t_x = tile_load(0)
            wst = weights_fn()
            wtoks = wst["tok"]
            last_mm = {"r": None, "k": None, "v": None}
            pending_y = tile_ln_apply(tile_ln_stats(0, t_x))
            (t_sr0, emit_k0) = tile_blend(0, pending_y, None)
            pending_spl = (t_sr0, emit_k0())
            lw_pump(wst, NCH)
            pending_ctx = None
            for (ti, (s0, Wt)) in enumerate(tiles):
                par = ti % 2
                t_y = pending_y
                t_spl = pending_spl

                t_sig = {}
                for ec in range(NCH):
                    pb = 4 + (ec % 4)
                    t_mm = P("tensor", lambda e, ec=ec, pb=pb, Wt=Wt:
                             mm3(e, PS[pb][:, 0:Wt], smap[0], ec * 128, 128,
                                 h1v, l1v, Wt),
                             after=[t_spl[0], wtoks[smap[0]],
                                    bank_tok[pb]],
                             inc="mm")
                    last_mm["r"] = t_mm
                    t_sig[ec] = P("scalar", lambda e, ec=ec, pb=pb, Wt=Wt:
                                  e.activation(t3v(ec, Wt), PS[pb][:, 0:Wt],
                                               AF.Sigmoid),
                                  after=[t_mm], inc="ac")
                    bank_tok[pb] = t_sig[ec]

                if ti == 0:
                    lw_pump(wst, NCH)
                if tail_fn is not None and ti == len(tiles) - 1:
                    tail_fn("r", last_mm)
                if ti + 1 < len(tiles):
                    t_xn = tile_load(ti + 1)
                    pending_ctx = tile_ln_stats(ti + 1, t_xn)

                t_kact = {}
                t_ksub = {}
                kmm_last = None
                for ec in range(NCH):
                    pb = 4 + (ec % 4)
                    t_mm = P("tensor", lambda e, ec=ec, pb=pb, Wt=Wt:
                             mm3(e, PS[pb][:, 0:Wt], smap[1], ec * 128, 128,
                                 h2v, l2v, Wt),
                             after=[t_spl[1], wtoks[smap[1]],
                                    bank_tok[pb]],
                             inc="mm")
                    kmm_last = t_mm
                    last_mm["k"] = t_mm
                    t_kact[ec] = P("scalar", lambda e, ec=ec, pb=pb, Wt=Wt: [
                        e.activation(t4v(ec, Wt), PS[pb][:, 0:Wt], AF.Relu),
                        e.drain(),
                        e.activation(t4v(ec, Wt), t4v(ec, Wt), AF.Square),
                        e.drain(),
                        e.activation(h4v(ec, Wt), t4v(ec, Wt), AF.Identity),
                    ][-1], after=[t_mm], inc="ac")
                    t_ksub[ec] = P("vector", lambda e, ec=ec, Wt=Wt:
                                   e.tensor_sub(l4v(ec, Wt), t4v(ec, Wt),
                                                h4v(ec, Wt)),
                                   after=[t_kact[ec]], inc="vx")
                    bank_tok[pb] = t_kact[ec]
                if ti == 0:
                    lw_pump(wst, NCH)
                if tail_fn is not None and ti == len(tiles) - 1:
                    tail_fn("k", last_mm)
                pending_k_fn = None
                if ti + 1 < len(tiles):
                    pending_y = tile_ln_apply(pending_ctx)
                    (t_srn, pending_k_fn) = tile_blend(ti + 1, pending_y,
                                                       kmm_last)

                t_combs = []
                t_cb_prev = None
                for ec in range(NCH):
                    pb = 4 + (ec % 4)
                    P("tensor", lambda e, ec=ec, pb=pb, Wt=Wt: [
                        e.matmul(PS[pb][:, 0:Wt],
                                 wsl(smap[2], 0, dc, ec * 128, 128),
                                 h4v(dc, Wt), start=(dc == 0), stop=False)
                        for dc in range(NCH)][-1],
                        after=[t_kact[NCH - 1], wtoks[smap[2]], bank_tok[pb]])
                    t_mm = P("tensor", lambda e, ec=ec, pb=pb, Wt=Wt: [
                        e.matmul(PS[pb][:, 0:Wt],
                                 wsl(smap[2], hf, dc, ec * 128, 128),
                                 xv_(dc, Wt),
                                 start=False,
                                 stop=(dc == NCH - 1 and hf == 1))
                        for (hf, xv_) in ((0, l4v), (1, h4v))
                        for dc in range(NCH)][-1],
                        after=[t_ksub[NCH - 1]], inc="mm")
                    last_mm["v"] = t_mm
                    t_cb = P("vector", lambda e, ec=ec, pb=pb, Wt=Wt, par=par: [
                        e.tensor_mul(scr[:, 0:Wt], t3v(ec, Wt), PS[pb][:, 0:Wt]),
                        e.drain(),
                        e.scalar_tensor_tensor(
                            xv(par, ec, Wt), scr[:, 0:Wt], 0.5, xv(par, ec, Wt),
                            ALU.mult, ALU.add),
                    ][-1], after=[t_mm, t_y, t_sig[ec], t_cb_prev], inc="vx")
                    bank_tok[pb] = t_cb
                    t_cb_prev = t_cb
                    t_combs.append(t_cb)
                if out_is_final:
                    dst = out_r[:, :, s0 - HALO: s0 - HALO + Wt]
                else:
                    dst = u_r[:, :, s0: s0 + Wt]
                t_spill_prev = P("gpsimd", lambda e, dst=dst, Wt=Wt, par=par: dma(
                    e, dst, xs3b[par][:, :, 0:Wt]),
                    after=t_combs, inc="sp", inc_amt=16)
                spill_by_par[par] = t_spill_prev
                if tail_fn is not None and ti == len(tiles) - 1:
                    tail_fn("v", last_mm)
                if pending_k_fn is not None:
                    pending_spl = (t_srn, pending_k_fn())
                G["v"] = t_combs[-1]
                G["s"] = t_kact[NCH - 1]
            last_mm["spill"] = t_spill_prev
            return last_mm

        open_scope()

        def ypc_A():
            pc0 = sml[:, 0:8]
            sc8 = sml[:, 16:24]
            t_l = P("sync", lambda e: dma_nc(e, pc0, x_r[:, :, 0:1].rearrange(
                "p c o -> p (c o)")), inc="ld", inc_amt=16)
            return emit_ln_col(pc0, pc0, sc8, [t_l, t_zero])

        tilesA = [(0, HALO)] + [(HALO + W * k, W) for k in range(OWN // W)]
        wstB_box = {}

        def tailA(ph, lm):
            if ph == "r":
                wstB_box["st"] = lw_begin(
                    [(0, w_ka), (2, w_va), (1, w_q)], gates={0: lm["r"]})
                lw_pump(wstB_box["st"], 3)
            elif ph == "k":
                wstB_box["st"]["gates"][1] = lm["k"]
                lw_pump(wstB_box["st"], 3)
            else:
                wstB_box["st"]["gates"][2] = lm["v"]
                lw_pump(wstB_box["st"], 2)
        ffn_sweep("A", tilesA, ypc_A, x_r, False,
                  lambda: lw_begin([(0, w_r), (1, w_k), (2, w_v)]),
                  tail_fn=tailA)
        wstB = wstB_box["st"]
        plan.barrier()
        close_scope()

        open_scope()

        usb = [stack_sbuf("us%d" % i, [128, NCH * W]) for i in range(2)]
        us3b = [u.ap().rearrange("p (c j) -> p c j", c=NCH) for u in usb]
        y2s = stack_sbuf("y2s", [128, NCH * W])
        y2h = [stack_sbuf("y2h%d" % i, [128, NCH * W], BF16) for i in range(2)]
        y2l = [stack_sbuf("y2l%d" % i, [128, NCH * W], BF16) for i in range(2)]
        sqln = stack_sbuf("sqln", [128, NCH * W])
        qev = stack_sbuf("qev", [128, NCH * W])
        ks = stack_sbuf("ks", [128, NCH * W])
        ktb = stack_sbuf("ktb", [128, (W // 128) * NH * 128])
        vtb = stack_sbuf("vtb", [128, (W // 128) * NH * 129])
        vtb3 = vtb.ap().rearrange("p (g f) -> p g f", f=129)
        qev3 = qev.ap().rearrange("p (c j) -> p c j", c=NCH)
        ks3 = ks.ap().rearrange("p (c j) -> p c j", c=NCH)

        def uvB(par, c, Wt):
            return usb[par][:, c * W: c * W + Wt]

        def y2hv(par):
            def f(c, Wt):
                return y2h[par][:, c * W: c * W + Wt]
            return f

        def y2lv(par):
            def f(c, Wt):
                return y2l[par][:, c * W: c * W + Wt]
            return f

        def sqvB(c, Wt):
            return sqln[:, c * W: c * W + Wt]

        def qvB(c, Wt):
            return qev[:, c * W: c * W + Wt]

        def kvB(c, Wt):
            return ks[:, c * W: c * W + Wt]

        y_by_par = [None, None]
        split_guard = [None, None]
        sp_tile = None
        t_acc_all = []
        deferred_q = []
        late_q_spills = []
        t_vones = P("vector", lambda e: e.memset(
            vtb3[:, :, 128:129], 1.0), inc="vx")

        def b_load(ti):
            (s0, Wt) = tilesA[ti]
            par = ti % 2
            return P("sync", lambda e, s0=s0, Wt=Wt, par=par: dma(
                e, us3b[par][:, :, 0:Wt], u_r[:, :, s0: s0 + Wt]),
                after=[y_by_par[par]], inc="ld%d" % par, inc_amt=16)

        def b_ln_stats(ti, t_u, extra):
            (s0, Wt) = tilesA[ti]
            par = ti % 2
            return emit_ln_stats(Wt, lambda c, Wt=Wt, par=par: uvB(par, c, Wt),
                                 lambda c, Wt=Wt: sqvB(c, Wt), [t_u] + extra)

        def b_ln_apply(ctx, ti):
            (s0, Wt) = tilesA[ti]
            par = ti % 2
            t_y = emit_ln_apply(ctx, lambda c, Wt=ctx[0]:
                                y2s[:, c * W: c * W + Wt])
            if Wt == W:
                t_spl = P("vector", lambda e, par=par: [
                    e.tensor_copy(y2h[par][:, :], y2s[:, :]),
                    e.drain(),
                    e.tensor_sub(y2l[par][:, :], y2s[:, :], y2h[par][:, :]),
                ][-1], after=[t_y, split_guard[par]], inc="vx")
            else:
                t_spl = P("vector", lambda e, par=par, Wt=Wt: ([
                    e.tensor_copy(y2hv(par)(c, Wt),
                                  y2s[:, c * W: c * W + Wt])
                    for c in range(NCH)]
                    + [e.drain()]
                    + [e.tensor_sub(y2lv(par)(c, Wt),
                                    y2s[:, c * W: c * W + Wt],
                                    y2hv(par)(c, Wt))
                       for c in range(NCH)])[-1],
                    after=[t_y, split_guard[par]], inc="vx")
            return t_spl

        t_u = b_load(0)
        wstC_box = {}
        wstD_box = {}
        wtoksB = wstB["tok"]
        pending_y = b_ln_apply(b_ln_stats(0, t_u, []), 0)
        bk_last = {}
        pending_ctx = None
        for (ti, (s0, Wt)) in enumerate(tilesA):
            par = ti % 2
            nb = Wt // 128
            b0 = s0 // 128
            t_y = pending_y
            y_by_par[par] = t_y
            t_ke = {}
            for ec in range(NCH):
                pb = 4 + (ec % 4)
                t_mm = P("tensor", lambda e, ec=ec, pb=pb, Wt=Wt, par=par:
                         mm3(e, PS[pb][:, 0:Wt], 0, ec * 128, 128,
                             y2hv(par), y2lv(par), Wt),
                         after=[t_y, wtoksB[0], bank_tok[pb]], inc="mm")
                bk_last["k"] = t_mm
                t_ke[ec] = P("vector", lambda e, ec=ec, pb=pb, Wt=Wt:
                             e.tensor_copy(kvB(ec, Wt), PS[pb][:, 0:Wt]),
                             after=[t_mm, sp_tile], inc="vx")
                bank_tok[pb] = t_ke[ec]
            if ti == 0:
                lw_pump(wstB, NCH)
            if ti + 1 < len(tilesA):
                t_un = b_load(ti + 1)
                pending_ctx = b_ln_stats(ti + 1, t_un, [])
            t_ve = {}
            for j in range(nb):
                for eh in range(2):
                    pb = 4 + ((j * 2 + eh) % 4)

                    def vfn(e, j=j, eh=eh, pb=pb, par=par):
                        combos = [(dc, sb, wh) for dc in range(NCH)
                                  for (sb, wh) in ((y2h, 0), (y2l, 0),
                                                   (y2h, 1))]
                        last = None
                        for (i, (dc, sb, wh)) in enumerate(combos):
                            last = e.matmul(
                                PS[pb][:, 0:512],
                                sb[par][:, dc * W + j * 128:
                                        dc * W + (j + 1) * 128],
                                wsl(2, wh, dc, eh * 512, 512),
                                start=(i == 0), stop=(i == len(combos) - 1))
                        return last
                    t_mm = P("tensor", vfn,
                             after=[t_y, wtoksB[2], bank_tok[pb]], inc="mm")
                    bk_last["v"] = t_mm
                    t_ve[(j, eh)] = P("vector", lambda e, j=j, eh=eh, pb=pb:
                                      e.tensor_copy(
                        vtb3[:, j * NH + eh * 4: j * NH + eh * 4 + 4, 0:128],
                        PS[pb][:, 0:512].rearrange("p (h f) -> p h f", f=128)),
                        after=[t_mm, sp_tile, t_vones], inc="vx")
                    bank_tok[pb] = t_ve[(j, eh)]
            if ti == 0:
                lw_pump(wstB, NCH)
            if ti == len(tilesA) - 1:
                wstC_box["st"] = lw_begin([(0, w_o)],
                                          gates={0: bk_last["k"]},
                                          lanes=(0, 1))
                lw_pump(wstC_box["st"], 4)
            t_kt = {}
            for (ix, (h, j)) in enumerate([(h, j) for h in range(NH)
                                           for j in range(nb)]):
                pb = 6 + (ix % 2)
                t_tr = P("tensor", lambda e, h=h, j=j, pb=pb: e.transpose(
                    PS[pb][:, 0:128],
                    ks[:, h * W + j * 128: h * W + (j + 1) * 128],
                    idn[:, :]),
                    after=[t_ke[h], bank_tok[pb]], inc="mm")
                t_kt[(h, j)] = P("vector", lambda e, h=h, j=j, pb=pb:
                                 e.tensor_copy(
                    ktb[:, (j * NH + h) * 128: (j * NH + h + 1) * 128],
                    PS[pb][:, 0:128]), after=[t_tr, sp_tile], inc="vx")
                bank_tok[pb] = t_kt[(h, j)]
            if s0 >= HALO:
                for h in range(NH):
                    pb = 6 + (h % 2)
                    t_tm = P("tensor", lambda e, h=h, nb=nb, pb=pb: [
                        e.matmul(PS[pb][:, 0:129],
                                 ktb[:, (j * NH + h) * 128:
                                     (j * NH + h + 1) * 128],
                                 vtb3[:, j * NH + h, 0:129],
                                 start=(j == 0), stop=(j == nb - 1))
                        for j in range(nb)][-1],
                        after=[bank_tok[pb]]
                        + [t_kt[(h, j)] for j in range(nb)]
                        + [t_ve[(j, eh)] for j in range(nb) for eh in range(2)],
                        inc="mm")
                    t_acc = P("vector", lambda e, h=h, pb=pb: e.tensor_add(
                        ccp[:, h * 129: (h + 1) * 129],
                        ccp[:, h * 129: (h + 1) * 129], PS[pb][:, 0:129]),
                        after=[t_tm, t_zero], inc="vx")
                    bank_tok[pb] = t_acc
                    t_acc_all.append(t_acc)
                G["v"] = t_acc
            else:
                G["v"] = t_kt[(NH - 1, nb - 1)]
            if ti == len(tilesA) - 1:
                lw_pump(wstC_box["st"], 4)
                wstD_box["st"] = lw_begin(
                    [(2, w_k), (1, w_r), (0, w_v)],
                    gates={2: bk_last["v"]}, lanes=(0, 1))
                lw_pump(wstD_box["st"], 4)
            if ti + 1 < len(tilesA):
                pending_y = b_ln_apply(pending_ctx, ti + 1)
            def emit_q(par, s0, Wt, t_y, qgate=None,
                       spq=("gpsimd", "sp")):
                t_qe = {}
                t_mm = None
                for ec in range(NCH):
                    pb = 4 + (ec % 4)
                    t_mm = P("tensor", lambda e, ec=ec, pb=pb, Wt=Wt, par=par:
                             mm3(e, PS[pb][:, 0:Wt], 1, ec * 128, 128,
                                 y2hv(par), y2lv(par), Wt),
                             after=[t_y, wtoksB[1], bank_tok[pb]], inc="mm")
                    t_qe[ec] = P("vector", lambda e, ec=ec, pb=pb, Wt=Wt:
                                 e.tensor_copy(qvB(ec, Wt), PS[pb][:, 0:Wt]),
                                 after=[t_mm, sp_tile, qgate], inc="vx")
                    bank_tok[pb] = t_qe[ec]
                t_qs = P(spq[0], lambda e, s0=s0, Wt=Wt: dma(
                    e, q_r[:, :, s0: s0 + Wt], qev3[:, :, 0:Wt]),
                    after=[t_qe[c] for c in range(NCH)], inc=spq[1],
                    inc_amt=16)
                return t_mm, t_qs

            if ti < len(tilesA) - 2:
                t_mmq, _ = emit_q(par, s0, Wt, t_y)
                split_guard[par] = t_mmq
            else:
                deferred_q.append((par, s0, Wt, t_y))
            if ti == len(tilesA) - 1:
                lw_pump(wstD_box["st"], 4)
            last_tile = ti == len(tilesA) - 1
            wsem = "spL" if last_tile else "sp"

            def sp_kf(s0=s0, Wt=Wt, t_ke=t_ke, wsem=wsem):
                return P("gpsimd", lambda e: dma(
                    e, kf_r[:, :, s0: s0 + Wt], ks3[:, :, 0:Wt]),
                    after=[t_ke[c] for c in range(NCH)], inc=wsem,
                    inc_amt=16)

            def sp_kt(b0=b0, nb=nb, t_kt=t_kt, wsem=wsem):
                return P("gpsimd", lambda e: dma(
                    e, kt_sp[:, b0 * NH * 128: (b0 + nb) * NH * 128],
                    ktb[:, 0: nb * NH * 128]),
                    after=[t_kt[(h, j)] for h in range(NH)
                           for j in range(nb)],
                    inc=wsem, inc_amt=16)

            def sp_vt(b0=b0, nb=nb, t_ve=t_ve, wsem=wsem):
                return P("gpsimd", lambda e: dma(
                    e, vt_sp[:, b0 * NH * 129: (b0 + nb) * NH * 129],
                    vtb[:, 0: nb * NH * 129]),
                    after=[t_vones]
                    + [t_ve[(j, eh)] for j in range(nb) for eh in range(2)],
                    inc=wsem, inc_amt=16)

            if last_tile:
                late_b_spills = [sp_kf, sp_kt, sp_vt]
                sp_tile = None
            else:
                sp_kf()
                sp_kt()
                sp_tile = sp_vt()

        t_pack = None
        for h in range(NH):
            t_pack = P("gpsimd", lambda e, h=h: dma(
                e, cc_in[:, h * 129: (h + 1) * 129],
                ccp[:, h * 129: (h + 1) * 129]),
                after=[t_acc_all[-NH + h], t_pack], inc="sp0", inc_amt=16)
        t_cc = P("gpsimd", lambda e: e.collective_compute(
            "AllGather", ALU.bypass, replica_groups=[list(range(N_CORES))],
            ins=[cc_in.ap().opt()], outs=[cc_out.ap().opt()]),
            after=[t_pack], inc="cc")
        t_qs = None
        t_mmq_last = None
        for (parq, s0q, Wtq, t_yq) in deferred_q:
            t_mmq_last, t_qs = emit_q(parq, s0q, Wtq, t_yq, qgate=t_qs)
            late_q_spills.append(t_qs)
        for fn in late_b_spills:
            late_q_spills.append(fn())
        wstC = wstC_box["st"]
        wstD = wstD_box["st"]
        wstD["gates"][1] = t_mmq_last
        plan.barrier()
        close_scope()

        open_scope()
        t_w = wstC["tok"][0]
        t_sz = P("vector", lambda e: e.memset(s0a[:, :], 0.0),
                 after=[G["v"]], inc="vx")
        gball = stack_sbuf("gball", [128, 4 * 2064])
        t_gls = []
        for half in range(4):
            t_gls.append(P("scalar", lambda e, half=half: dma(
                e, gball[:, half * 2064: (half + 1) * 2064]
                .rearrange("p (j f) -> p j f", j=2),
                cc_out.ap().rearrange("(j p) f -> p j f", p=128)
                [:, half * 2: (half + 1) * 2, :]).then_inc(
                    sems["gg%d" % half], 16),
                after=[t_cc], inc="gg%d" % half, inc_amt=16, self_inc=True))
        t_acc_prev = t_sz
        for half in range(4):
            for j in range(2):
                t_acc_prev = P("vector", lambda e, j=j, half=half:
                               e.scalar_tensor_tensor(
                    s0a[:, :],
                    gball[:, half * 2064 + j * 1032:
                          half * 2064 + (j + 1) * 1032],
                    pmk[:, half * 2 + j: half * 2 + j + 1], s0a[:, :],
                    ALU.mult, ALU.add),
                    after=[t_gls[half], t_acc_prev], inc="vx")
        t_s0 = t_acc_prev

        qb = [stack_sbuf("qb%d" % i, [128, NH * 128]) for i in range(2)]
        kfb = [stack_sbuf("kfb%d" % i, [128, NH * 128]) for i in range(2)]
        ktc = [stack_sbuf("ktc%d" % i, [128, NH * 128]) for i in range(2)]
        vtc = [stack_sbuf("vtc%d" % i, [128, NH * 129]) for i in range(2)]
        ubf = [stack_sbuf("ubf%d" % i, [128, NCH * 128]) for i in range(2)]
        msT = [stack_sbuf("msT%d" % i, [128, 128]) for i in range(2)]
        o1b = [stack_sbuf("o1b%d" % i, [128, 128]) for i in range(2)]
        oscr = [stack_sbuf("oscr%d" % i, [128, 128]) for i in range(2)]
        ofmh = stack_sbuf("ofmh", [128, NH * 128], BF16)
        ofml = stack_sbuf("ofml", [128, NH * 128], BF16)
        v2b = stack_sbuf("v2b", [128, NCH * 128])
        def S129(h):
            return s0a[:, h * 129: (h + 1) * 129]
        UPD = [PS[0], PS[1]]
        SC = [PS[2], PS[3]]
        OD = [PS[4], PS[5]]
        TR = [PS[6], PS[7]]
        DABS = [sml[:, 26:27], sml[:, 32:33]]
        DMAX = [sml[:, 27:28], sml[:, 33:34]]
        DRCP = [sml[:, 28:29], sml[:, 34:35]]
        DSS = [sml[:, 29:30], sml[:, 35:36]]
        DSD = [sml[:, 30:31], sml[:, 36:37]]
        DRMS = [sml[:, 31:32], sml[:, 37:38]]

        def emit_loads(b, after_toks):
            par = b % 2
            s0 = b * BLK
            ldp = "ld%d" % par

            def fn(e):
                dma(e, qb[par].ap().rearrange("p (c j) -> p c j", c=NCH),
                    q_r[:, :, s0: s0 + BLK]).then_inc(sems[ldp], 16)
                dma(e, kfb[par].ap().rearrange("p (c j) -> p c j", c=NCH),
                    kf_r[:, :, s0: s0 + BLK]).then_inc(sems[ldp], 16)
                dma(e, ktc[par][:, :],
                    kt_sp[:, b * NH * 128: (b + 1) * NH * 128]
                    ).then_inc(sems[ldp], 16)
                dma(e, vtc[par][:, :],
                    vt_sp[:, b * NH * 129: (b + 1) * NH * 129]
                    ).then_inc(sems[ldp], 16)
                last = dma(e, ubf[par].ap().rearrange("p (c j) -> p c j",
                                                      c=NCH),
                           u_r[:, :, s0: s0 + BLK])
                last.then_inc(sems[ldp], 16)
                return last
            extra = late_q_spills if (b + 1) * BLK > EXT - 2 * W else []
            return P("sync", fn, after=after_toks + extra, inc=ldp,
                     inc_amt=5 * 16, self_inc=True)

        state_tok = [t_s0] * NH
        hb = {
            "sadd": [None, None],
            "mask": [None, None],
            "o1": [None, None],
            "abs": [None, None],
            "ofm": [None, None],
            "o2": [None, None],
            "rcp": [None, None],
            "troscr": [None, None],
        }
        t_loads = {0: emit_loads(0, []), 1: emit_loads(1, [])}
        t_consumed = {0: [], 1: []}
        t_spv2_prev = None

        def hsl(buf, h):
            return buf[:, h * 128: (h + 1) * 128]

        def vsl(buf, h):
            return buf[:, h * 129: h * 129 + 129]

        for b in range(NBLK):
            par = b % 2
            s0 = b * BLK
            ld1 = t_loads[b]
            q_, kf_, kt_, vt_, u_ = qb[par], kfb[par], ktc[par], vtc[par], ubf[par]

            t_o_h = {}
            t_tr_h = {}
            t_ofm = {}

            halo = (b == 0)

            def emit_upd_sc(h):
                hp = h % 2
                t_upd = P("tensor", lambda e, kt_=kt_, vt_=vt_, h=h, hp=hp:
                          e.matmul(UPD[hp][:, 0:129], hsl(kt_, h),
                                   vsl(vt_, h), start=True, stop=True),
                          after=[ld1, hb["sadd"][hp]], inc="mm")
                sdep = state_tok[h]
                if halo:
                    t_sub = P("vector", lambda e, h=h, hp=hp: e.tensor_sub(
                        S129(h), S129(h), UPD[hp][:, 0:129]),
                        after=[t_upd, sdep, t_s0], inc="vx")
                    sdep = t_sub
                t_sc = P("tensor", lambda e, kf_=kf_, q_=q_, h=h, hp=hp:
                         e.matmul(SC[hp][:, 0:128], hsl(kf_, h), hsl(q_, h),
                                  start=True, stop=True),
                         after=[ld1, hb["mask"][hp]], inc="mm")
                t_ms = P("vector", lambda e, hp=hp: e.tensor_mul(
                    msT[hp][:, :], SC[hp][:, 0:128], triu[:, :]),
                    after=[t_sc], inc="vx")
                hb["mask"][hp] = t_ms
                return (t_upd, t_ms, sdep)

            def emit_o(h, toks):
                hp = h % 2
                (t_upd, t_ms, sdep) = toks
                t_o = P("tensor", lambda e, q_=q_, vt_=vt_, h=h, hp=hp: [
                    e.matmul(OD[hp][:, 0:129], msT[hp][:, :], vsl(vt_, h),
                             start=True, stop=False),
                    e.matmul(OD[hp][:, 0:129], hsl(q_, h), S129(h),
                             start=False, stop=True),
                ][-1], after=[t_ms, ld1, sdep, hb["o1"][hp], hb["abs"][hp]],
                    inc="mm")
                t_o_h[h] = t_o
                t_sadd = P("vector", lambda e, h=h, hp=hp: e.tensor_add(
                    S129(h), S129(h), UPD[hp][:, 0:129]),
                    after=[t_o, t_upd], inc="vx")
                hb["sadd"][hp] = t_sadd
                state_tok[h] = t_sadd
                t_abs = P("scalar", lambda e, hp=hp: e.activation(
                    DABS[hp], OD[hp][:, 128:129], AF.Abs),
                    after=[t_o], inc="ac")
                hb["abs"][hp] = t_abs
                t_rcp = P("vector", lambda e, hp=hp: [
                    e.tensor_scalar_max(DMAX[hp], DABS[hp], 1.0),
                    e.drain(),
                    e.reciprocal(DRCP[hp], DMAX[hp]),
                ][-1], after=[t_abs, hb["rcp"][hp]], inc="vx")
                hb["rcp"][hp] = t_rcp
                t_o1 = P("scalar", lambda e, hp=hp: e.activation(
                    o1b[hp][:, :], OD[hp][:, 0:128], AF.Copy,
                    scale=DRCP[hp]),
                    after=[t_o, t_rcp, hb["o2"][hp]], inc="ac")
                hb["o1"][hp] = t_o1
                t_ss = P("scalar", lambda e, hp=hp: e.activation(
                    oscr[hp][:, :], o1b[hp][:, :], AF.Square,
                    accum_out=DSS[hp]),
                    after=[t_o1, hb["troscr"][hp]], inc="ac")
                t_sd = P("scalar", lambda e, hp=hp: e.activation(
                    DSD[hp], DSS[hp], AF.Sqrt,
                    bias=cst[:, 1:2], scale=cst[:, 3:4]),
                    after=[t_ss], inc="ac")
                t_rms = P("vector", lambda e, hp=hp: e.reciprocal(
                    DRMS[hp], DSD[hp]), after=[t_sd], inc="vx")
                t_o2 = P("vector", lambda e, hp=hp: e.scalar_tensor_tensor(
                    oscr[hp][:, :], o1b[hp][:, :], DRMS[hp], nrmw[:, :],
                    ALU.mult, ALU.mult),
                    after=[t_rms, t_ss, hb["troscr"][hp]], inc="vx")
                hb["o2"][hp] = t_o2
                return t_o2

            def emit_tr(h, t_o2):
                hp = h % 2
                t_tr = P("tensor", lambda e, hp=hp: e.transpose(
                    TR[hp][:, 0:128], oscr[hp][:, :], idn[:, :]),
                    after=[t_o2, hb["ofm"][hp]], inc="mm")
                t_tr_h[h] = t_tr
                hb["troscr"][hp] = t_tr
                t_oh = P("scalar", lambda e, h=h, hp=hp: e.activation(
                    hsl(ofmh, h), TR[hp][:, 0:128], AF.Copy),
                    after=[t_tr], inc="ac")
                t_ofm[h] = P("vector", lambda e, h=h, hp=hp: e.tensor_sub(
                    hsl(ofml, h), TR[hp][:, 0:128], hsl(ofmh, h)),
                    after=[t_oh], inc="vx")
                hb["ofm"][hp] = t_ofm[h]

            toks = {}
            o2s = {}
            for h in range(NH):
                toks[h] = emit_upd_sc(h)
                if h >= 1:
                    o2s[h - 1] = emit_o(h - 1, toks[h - 1])
                if h >= 2:
                    emit_tr(h - 2, o2s[h - 2])
            o2s[NH - 1] = emit_o(NH - 1, toks[NH - 1])
            emit_tr(NH - 2, o2s[NH - 2])
            emit_tr(NH - 1, o2s[NH - 1])

            t_v2 = {}
            for dc in range(NCH):
                hp = dc % 2

                def wofn(e, dc=dc, hp=hp):
                    combos = [(ec, sb, wh) for ec in range(NCH)
                              for (sb, wh) in ((ofmh, 0), (ofml, 0),
                                               (ofmh, 1))]
                    last = None
                    for (i, (ec, sb, wh)) in enumerate(combos):
                        last = e.matmul(OD[hp][:, 0:128],
                                        wsl(0, wh, ec, dc * 128, 128),
                                        hsl(sb, ec), start=(i == 0),
                                        stop=(i == len(combos) - 1))
                    return last
                t_mm = P("tensor", wofn,
                         after=[t_w, hb["o1"][hp], hb["abs"][hp],
                                t_v2.get(dc - 2)]
                         + [t_ofm[h] for h in range(NH)],
                         inc="mm")
                G["wo_last"] = t_mm
                t_v2[dc] = P("vector", lambda e, dc=dc, hp=hp, u_=u_:
                             e.tensor_add(
                    v2b[:, dc * 128: (dc + 1) * 128],
                    u_[:, dc * 128: (dc + 1) * 128], OD[hp][:, 0:128]),
                    after=[t_mm, ld1, t_spv2_prev], inc="vx")
            hb["o1"] = [t_v2[NCH - 2], t_v2[NCH - 1]]
            t_spv2_prev = P("gpsimd", lambda e, s0=s0: dma(
                e, v2_r[:, :, s0: s0 + BLK],
                v2b.ap().rearrange("p (c j) -> p c j", c=NCH)),
                after=[t_v2[dc] for dc in range(NCH)], inc="sp", inc_amt=16)
            G["v"] = t_v2[NCH - 1]
            t_consumed[par] = [t_spv2_prev] + [t_ofm[h] for h in range(NH)] \
                + [t_v2[dc] for dc in range(NCH)] + [t_o_h[NH - 1]]
            if b + 2 < NBLK:
                t_loads[b + 2] = emit_loads(b + 2, t_consumed[par])
            if b == 1:
                lw_pump(wstD, NCH)
        plan.barrier()
        close_scope()

        open_scope()

        def ypc_D():
            pc0 = sml[:, 0:8]
            sc8 = sml[:, 16:24]
            t_l = P("sync", lambda e: dma_nc(
                e, pc0, v2_r[:, :, HALO - 1: HALO].rearrange(
                    "p c o -> p (c o)")), inc="ld", inc_amt=16)
            return emit_ln_col(pc0, pc0, sc8, [t_l])

        tilesD = [(HALO + W * k, W) for k in range(OWN // W)]
        wstD["gates"][0] = G["wo_last"]
        ffn_sweep("D", tilesD, ypc_D, v2_r, True,
                  lambda: wstD, smap=(1, 2, 0))
        plan.barrier()
        close_scope()

        plan.emit(sems, engines)

    return nc


_NC_CACHE = None


def _get_nc():
    global _NC_CACHE
    if _NC_CACHE is None:
        _NC_CACHE = build_nc()
    return _NC_CACHE


def make_in_maps(x, Wr, Wk, Wv, Wq, Wka, Wva, Wo, mu_r, mu_k, norm_w):
    def T(a):
        return np.ascontiguousarray(np.asarray(a, dtype=np.float32).T)

    w_r, w_k, w_v = T(Wr), T(Wk), T(Wv)
    w_q, w_ka, w_va, w_o = T(Wq), T(Wka), T(Wva), T(Wo)

    mu_r = np.asarray(mu_r, dtype=np.float32)
    mu_k = np.asarray(mu_k, dtype=np.float32)
    muv = np.empty((128, 16), np.float32)
    muv[:, 0:8] = mu_r.reshape(NCH, 128).T
    muv[:, 8:16] = mu_k.reshape(NCH, 128).T

    nw = np.asarray(norm_w, dtype=np.float32)
    nrmw = np.ascontiguousarray(np.broadcast_to(nw[None, :], (128, 128)),
                                dtype=np.float32)
    triu = np.triu(np.ones((128, 128), np.float32))
    idn = np.eye(128, dtype=np.float32)
    cst = np.zeros((128, 8), np.float32)
    cst[:, 0] = 1.0
    cst[:, 1] = EPS
    cst[:, 2] = 1.0 / D
    cst[:, 3] = 1.0 / 128
    cst[:, 4] = 0.5
    onesr = np.ones((1, 128), np.float32)

    in_maps = []
    for i in range(N_CORES):
        lo = i * OWN - (HALO + 1)
        x_ext = np.zeros((XCOLS, D), np.float32)
        src_lo = max(lo, 0)
        x_ext[src_lo - lo: XCOLS, :] = x[src_lo: (i + 1) * OWN, :]
        pmk = np.zeros((128, 8), np.float32)
        pmk[:, :i] = 1.0
        in_maps.append({
            "x": np.ascontiguousarray(x_ext.T),
            "w_r": w_r, "w_k": w_k, "w_v": w_v,
            "w_q": w_q, "w_ka": w_ka, "w_va": w_va, "w_o": w_o,
            "muv": muv, "nrmw": nrmw, "triu": triu, "idn": idn,
            "cst": cst, "onesr": onesr, "pmk": pmk,
        })
    return in_maps


def kernel(L0_attn_hidden_states, L0_attn_Wq, L0_attn_Wk, L0_attn_Wv,
           L0_attn_Wo, L0_attn_norm_w, L0_ffn_W_r, L0_ffn_W_k, L0_ffn_W_v,
           L0_ffn_mu_r, L0_ffn_mu_k):
    x = np.asarray(L0_attn_hidden_states, dtype=np.float32).reshape(S_TOTAL, D)
    in_maps = make_in_maps(
        x, L0_ffn_W_r, L0_ffn_W_k, L0_ffn_W_v,
        L0_attn_Wq, L0_attn_Wk, L0_attn_Wv, L0_attn_Wo,
        L0_ffn_mu_r, L0_ffn_mu_k, L0_attn_norm_w)
    nc = _get_nc()
    res = run_bass_kernel_spmd(nc, in_maps, core_ids=list(range(N_CORES)))
    out = np.empty((S_TOTAL, D), np.float32)
    for i in range(N_CORES):
        out[i * OWN: (i + 1) * OWN, :] = res.results[i]["out"].T
    return out.reshape(1, S_TOTAL, D)
